# version 03
# baseline (speedup 1.0000x reference)
import numpy as np
from contextlib import ExitStack

import concourse.bass as bass
import concourse.mybir as mybir
from concourse.bass_utils import run_bass_kernel_spmd

F32 = mybir.dt.float32
BF16 = mybir.dt.bfloat16
U32 = mybir.dt.uint32
AF = mybir.ActivationFunctionType
ALU = mybir.AluOpType
AX = mybir.AxisListType

C = 1024
H = 16
E = 8
LN_EPS = 1e-5
FM_EPS16 = 1e-6 / 16.0
DEN_EPS = 1e-6
DN = float(64.0 ** (-0.25))
INV16 = 1.0 / 16.0

DMA_SEMS = frozenset("XWEOPD")
SEM_NAMES = ["T", "V", "S", "G", "X", "W", "E", "O", "P", "D"]


class Sched:
    def __init__(self):
        self.cnt = {s: 0 for s in SEM_NAMES}
        self.ops = {e: [] for e in "TVSGY"}

    def add(self, eng, fn, waits=(), inc=None, amt=1):
        ws = []
        for w in waits:
            if w is None:
                continue
            s, v = w
            if v and v > 0:
                ws.append((s, v))
        a = 0
        val = None
        if inc is not None:
            a = 16 if inc in DMA_SEMS else amt
            self.cnt[inc] += a
            val = self.cnt[inc]
        self.ops[eng].append((fn, ws, inc, a))
        return val

    def emit(self, eng_obj, eng_name, sems):
        maxw = {}
        for fn, ws, inc, a in self.ops[eng_name]:
            for s, v in ws:
                if s == eng_name and s == "T":
                    continue
                if maxw.get(s, 0) >= v:
                    continue
                maxw[s] = v
                eng_obj.wait_ge(sems[s], v)
            instr = fn(eng_obj)
            if inc is not None:
                instr.then_inc(sems[inc], a)


def build_program(NB1=16, NB2=8, NE=8, dbg=False):
    nc = bass.Bass("TRN2", target_bir_lowering=False)

    x_d = nc.dram_tensor("x_seq", [8192, C], F32, kind="ExternalInput").ap()
    wqkv_d = nc.dram_tensor("w_qkv", [C, 3 * C], F32, kind="ExternalInput").ap()
    bqkv_d = nc.dram_tensor("b_qkv", [3 * C], F32, kind="ExternalInput").ap()
    wout_d = nc.dram_tensor("w_out", [C, C], F32, kind="ExternalInput").ap()
    bout_d = nc.dram_tensor("b_out", [C], F32, kind="ExternalInput").ap()
    proj_d = nc.dram_tensor("proj", [H, 256, 64], F32, kind="ExternalInput").ap()
    gate_d = nc.dram_tensor("gate_w", [E, C], F32, kind="ExternalInput").ap()
    expw_d = nc.dram_tensor("expert_w", [E, C, C], F32, kind="ExternalInput").ap()
    out_d = nc.dram_tensor("out", [4096, C], F32, kind="ExternalOutput").ap()
    wet_d = nc.dram_tensor("wet_scratch", [E, C, C], BF16, kind="Internal").ap()

    es = ExitStack()
    S = Sched()

    def sb(name, shape, dt=F32):
        return es.enter_context(nc.sbuf_tensor(name, shape, dt))

    def psum(name, shape, dt=F32):
        return es.enter_context(nc.psum_tensor(name, shape, dt))

    with es:
        WA = sb("WA", [128, 8, 8, 128], BF16)
        WB = sb("WB", [128, 8, C], BF16)
        PD = sb("PD", [128, 8, 512], BF16)
        GT = sb("GT", [128, 8, 8], BF16)
        BQc = sb("BQc", [128, 8], F32)
        BKc = sb("BKc", [128, 8], F32)
        BVr = sb("BVr", [1, C], BF16)
        BOr = sb("BOr", [1, C], BF16)
        ONE1 = sb("ONE1", [1, 128], BF16)
        IDB = sb("IDB", [128, 128], BF16)
        IDF = sb("IDF", [128, 128], F32)
        IO8 = sb("IO8", [128, 8], F32)
        STs = sb("STs", [128, 64], F32)
        STn = sb("STn", [128, 64], F32)
        KVB = sb("KVB", [128, 2, H, 68], BF16)
        GW8 = sb("GW8", [8, C], BF16)

        XT = sb("XT", [128, 4, C], F32)
        XNB = sb("XNB", [128, 2, C], BF16)
        XNF = sb("XNF", [128, 8, 4, 128], BF16)
        KF = sb("KF", [128, 4, 512], BF16)
        VE = sb("VE", [128, 4, 8, 68], BF16)
        KPF = sb("KPF", [128, 2, 512], F32)
        KPB = sb("KPB", [128, 8192], BF16)
        KVS = sb("KVS", [128, 4096], F32)
        QF = sb("QF", [128, 8, 512], BF16)
        X1B = sb("X1B", [128, 4, C], F32)
        XN2 = sb("XN2", [128, 4, C], F32)
        YN = sb("YN", [128, C], F32)
        OH = sb("OH", [128, 4, 8], F32)
        WET = sb("WET", [128, 2, 8, 512], BF16)
        OTMB = sb("OTMB", [128, 16, 64], BF16)
        DEN = sb("DEN", [128, 16], F32)
        RDEN = sb("RDEN", [128, 16], F32)
        MXZ = sb("MXZ", [128, 4], F32)
        GMX = sb("GMX", [128, 4, 8], F32)
        GIX = sb("GIX", [128, 4, 8], U32)
        IXF = sb("IXF", [128, 4], F32)

        STMP = sb("STMP", [128, 2, 512], F32)
        BNS = sb("BNS", [128, 4, 12], F32)
        MV = sb("MV", [128, 4, 2, 2], F32)
        LT0 = sb("LT0", [128, 4], F32)
        LT1 = sb("LT1", [128, 4], F32)
        LT2 = sb("LT2", [128, 4], F32)
        LT3 = sb("LT3", [128, 4], F32)
        LSc = sb("LSc", [128, 4], F32)
        LNm = sb("LNm", [128, 4], F32)

        TRB = psum("TRB", [128, 1024], BF16)
        KPP = psum("KPP", [128, 512], F32)
        VPP = psum("VPP", [128, 512], F32)
        ZKP = psum("ZKP", [128, 512], F32)
        KV0f = psum("KV0f", [128, 1024], F32)
        KV1f = psum("KV1f", [128, 1024], F32)
        kv_of = {0: KV0f, 1: KV1f}

        sems = {}
        for name in SEM_NAMES:
            sems[name] = es.enter_context(nc.semaphore("sem" + name))

        def rt(eng):
            return [(eng, S.cnt[eng])]

        def ln_bnstats(col, src0, src1, waits):
            S.add("V", lambda v, a=src0, c=col: v.bn_stats(BNS[:, c, 0:6], a),
                  waits=waits, inc="V")
            return S.add("V", lambda v, a=src1, c=col: v.bn_stats(BNS[:, c, 6:12], a),
                         inc="V")

        def ln_aggr(col):
            S.add("V", lambda v, c=col: v.bn_aggr(MV[:, c, 0, :], BNS[:, c, 0:6]), inc="V")
            return S.add("V", lambda v, c=col: v.bn_aggr(MV[:, c, 1, :], BNS[:, c, 6:12]),
                         inc="V")

        def ln_combine(c0, n):
            cs = slice(c0, c0 + n)
            m0 = MV[:, cs, 0:1, 0:1]
            m1 = MV[:, cs, 1:2, 0:1]
            v0 = MV[:, cs, 0:1, 1:2]
            v1 = MV[:, cs, 1:2, 1:2]
            S.add("V", lambda v: v.tensor_tensor(LT0[:, cs], m0, m1, ALU.subtract),
                  waits=rt("V"), inc="V")
            S.add("V", lambda v: v.tensor_tensor(LT1[:, cs], v0, v1, ALU.add), inc="V")
            S.add("V", lambda v: v.tensor_tensor(LT3[:, cs], m0, m1, ALU.add), inc="V")
            S.add("V", lambda v: v.tensor_tensor(LT2[:, cs], LT0[:, cs], LT0[:, cs], ALU.mult),
                  waits=rt("V"), inc="V")
            S.add("V", lambda v: v.tensor_scalar(LT1[:, cs], LT1[:, cs], 0.5, LN_EPS,
                                                 ALU.mult, ALU.add), inc="V")
            vv = S.add("V", lambda v: v.scalar_tensor_tensor(LT0[:, cs], LT2[:, cs], 0.25,
                                                             LT1[:, cs], ALU.mult, ALU.add),
                       waits=rt("V"), inc="V")
            sv = S.add("S", lambda s: s.activation(LT2[:, cs], LT0[:, cs], AF.Ln,
                                                   bias=0.0, scale=1.0),
                       waits=[("V", vv)], inc="S")
            sv2 = S.add("S", lambda s: s.activation(LSc[:, cs], LT2[:, cs], AF.Exp,
                                                    bias=0.0, scale=-0.5),
                        waits=[("S", sv)], inc="S")
            S.add("V", lambda v: v.tensor_tensor(LT3[:, cs], LT3[:, cs], LSc[:, cs], ALU.mult),
                  waits=[("S", sv2)], inc="V")
            return S.add("V", lambda v: v.tensor_scalar(LNm[:, cs], LT3[:, cs], -0.5, None,
                                                        ALU.mult),
                         waits=rt("V"), inc="V")

        pBQ = S.add("Y", lambda y: y.dma_start(
            out=BQc[:, :], in_=bqkv_d[0:1024].rearrange("(jc p) -> p jc", p=128),
            allow_slow_non_contiguous=True), inc="P")
        pBK = S.add("Y", lambda y: y.dma_start(
            out=BKc[:, :], in_=bqkv_d[1024:2048].rearrange("(jc p) -> p jc", p=128),
            allow_slow_non_contiguous=True), inc="P")

        wWK = None
        for cc in range(8):
            wWK = S.add("G", lambda g, cc=cc: g.dma_start(
                out=WA[:, cc, :, :].rearrange("p a b -> p (a b)"),
                in_=wqkv_d[cc * 128:(cc + 1) * 128, 1024:2048]), inc="W")
        wWV = None
        for cc in range(8):
            wWV = S.add("G", lambda g, cc=cc: g.dma_start(
                out=WB[:, cc, :], in_=wqkv_d[cc * 128:(cc + 1) * 128, 2048:3072]), inc="W")
        wBV = S.add("G", lambda g: g.dma_start(out=BVr[:, :], in_=bqkv_d[None, 2048:3072]),
                    inc="W")
        wGW = S.add("G", lambda g: g.dma_start(out=GW8[:, :], in_=gate_d[:, :]), inc="W")

        S.add("G", lambda g: g.memset(ONE1[:, :], 1.0), inc="G")
        S.add("G", lambda g: g.iota(XT[:, 0, 0:128], [[1, 128]], base=0, channel_multiplier=0,
                                    allow_small_or_imprecise_dtypes=True), inc="G")
        S.add("G", lambda g: g.iota(XT[:, 0, 128:256], [[0, 128]], base=0, channel_multiplier=1,
                                    allow_small_or_imprecise_dtypes=True), inc="G")
        gIot = S.add("G", lambda g: g.iota(IO8[:, :], [[1, 8]], base=0, channel_multiplier=0,
                                           allow_small_or_imprecise_dtypes=True), inc="G")
        gPD0 = S.add("G", lambda g: g.memset(PD[:, :, :], 0.0), inc="G")
        gVE1 = S.add("G", lambda g: g.memset(VE[:, :, :, 64:68], 1.0), inc="G")

        S.add("V", lambda v: v.tensor_tensor(IDF[:, :], XT[:, 0, 0:128], XT[:, 0, 128:256],
                                             ALU.is_equal),
              waits=[("G", gIot)], inc="V")
        vIDb = S.add("V", lambda v: v.tensor_copy(IDB[:, :], IDF[:, :]),
                     waits=rt("V"), inc="V")

        trb_free = ("V", 0)
        vPD = 0
        for h in range(H):
            wPJ = S.add("G", lambda g, h=h: g.dma_start(
                out=XNB[:, 1, 0:128].rearrange("p (a b) -> p a b", a=2),
                in_=proj_d[h].rearrange("(mc p) d -> p mc d", p=128)),
                waits=[("V", vPD)], inc="W")
            gSc = S.add("G", lambda g: g.tensor_scalar(XNB[:, 1, 0:128], XNB[:, 1, 0:128],
                                                       DN, None, ALU.mult),
                        waits=[("W", wPJ)], inc="G")
            tTR = None
            for mc in range(2):
                tTR = S.add("T", lambda t, mc=mc: t.transpose(
                    TRB[0:64, mc * 128:(mc + 1) * 128],
                    XNB[:, 1, mc * 64:(mc + 1) * 64], IDB[:, :]),
                    waits=[("G", gSc), ("V", vIDb), trb_free] if mc == 0 else [],
                    inc="T" if mc == 1 else None)
            hp, lo = h // 2, h % 2
            vPD = S.add("V", lambda v, hp=hp, lo=lo: v.tensor_copy(
                PD[lo * 64:(lo + 1) * 64, hp, lo * 256:(lo + 1) * 256], TRB[0:64, 0:256]),
                waits=[("T", tTR), ("G", gPD0)], inc="V")
            trb_free = ("V", vPD)

        tG = None
        for cc in range(8):
            tG = S.add("T", lambda t, cc=cc: t.transpose(
                TRB[:, 256 + cc * 8:256 + (cc + 1) * 8],
                GW8[0:8, cc * 128:(cc + 1) * 128], IDB[0:8, 0:8]),
                waits=[("W", wGW), trb_free] if cc == 0 else [],
                inc="T" if cc == 7 else None)
        vGT = S.add("V", lambda v: v.tensor_copy(
            GT[:, :, :].rearrange("p a b -> p (a b)"), TRB[:, 256:320]),
            waits=[("T", tG)], inc="V")
        trb_free = ("V", vGT)

        WETB = KPB.rearrange("p (hc f) -> p hc f", hc=8)
        xnb_free = {0: ("T", 0), 1: ("T", S.cnt["T"])}
        yWET = 0
        for e in range(NE):
            vEv = None
            for fc in range(8):
                slot = fc % 2
                wEW = S.add("G", lambda g, e=e, slot=slot, fc=fc: g.dma_start(
                    out=XNB[:, slot, :], in_=expw_d[e, fc * 128:(fc + 1) * 128, :]),
                    waits=[xnb_free[slot]], inc="W")
                dE = None
                for hc in range(8):
                    dE = S.add("Y", lambda y, slot=slot, hc=hc, fc=fc: y.dma_start_transpose(
                        out=WETB[:, hc, fc * 128:(fc + 1) * 128],
                        in_=XNB[:, slot, hc * 128:(hc + 1) * 128]),
                        waits=[("W", wEW)] + ([("P", yWET)] if fc == 0 else []),
                        inc="D")
                xnb_free[slot] = ("D", dE)
                vEv = dE
            yWET = S.add("Y", lambda y, e=e: y.dma_start(
                out=wet_d[e].rearrange("(hc p) f -> p hc f", p=128), in_=WETB[:, :, :]),
                waits=[("D", vEv)], inc="P")
        yWET_done = S.cnt["P"]
        kpb_free = ("P", yWET_done)

        xt_free = {0: [("V", vIDb)], 1: [("S", 0)], 2: [("S", 0)], 3: [("S", 0)]}
        xnf_free = ("T", 0)
        kf_free = ("T", 0)
        ve_free = ("T", 0)
        kpf_free = {0: ("G", 0), 1: ("G", 0)}
        kpp_free = [("V", 0)]
        vpp_free = [("V", 0)]
        zk_free = {0: [("V", 0), ("S", 0)], 1: [("V", 0), ("S", 0)]}
        sweep_evac = 0

        for sweep in range(2):
            kv_started = set()
            last_kv = 0
            for bt in range(NB1):
                sub_info = []
                for sub in range(4):
                    ts = bt * 4 + sub
                    slot = sub
                    xw = S.add("Y", lambda y, ts=ts, slot=slot: y.dma_start(
                        out=XT[:, slot, :], in_=x_d[ts * 128:(ts + 1) * 128, :]),
                        waits=xt_free[slot] +
                              ([("W", wBV), ("P", pBK)] if (sweep == 0 and bt == 0) else []),
                        inc="X")
                    vbn = None
                    if sweep == 0:
                        vbn = ln_bnstats(sub, XT[:, slot, 0:512], XT[:, slot, 512:1024],
                                         waits=[("X", xw)])
                    sub_info.append((slot, xw, ts, vbn))
                if sweep == 0:
                    for sub in range(4):
                        ln_aggr(sub)
                    ln_combine(0, 4)
                    S.add("V", lambda v, bt=bt: v.tensor_copy(
                        STs[:, bt * 4:(bt + 1) * 4], LSc[:, 0:4]), waits=rt("V"), inc="V")
                    vstat = S.add("V", lambda v, bt=bt: v.tensor_copy(
                        STn[:, bt * 4:(bt + 1) * 4], LNm[:, 0:4]), inc="V")
                else:
                    vstat = 0

                xnf_parts = []
                for sub in range(4):
                    slot, xw, ts, vbn = sub_info[sub]
                    ns = sub % 2
                    sN = S.add("S", lambda s, slot=slot, ns=ns, ts=ts: s.activation(
                        XNB[:, ns, :], XT[:, slot, :], AF.Identity,
                        bias=STn[:, ts:ts + 1], scale=STs[:, ts:ts + 1]),
                        waits=[("X", xw), ("V", vstat), xnb_free[ns]], inc="S")
                    xt_free[slot] = [("S", sN)] + ([("V", vbn)] if vbn else [])
                    dT = None
                    for ccg in range(8):
                        dT = S.add("Y", lambda y, ns=ns, ccg=ccg, sub=sub: y.dma_start_transpose(
                            out=XNF[:, ccg, sub, :],
                            in_=XNB[:, ns, ccg * 128:(ccg + 1) * 128]),
                            waits=[("S", sN), xnf_free] if ccg == 0 else [],
                            inc="D")
                    xnb_free[ns] = ("D", dT)
                    xnf_parts.append(("D", dT))

                kf_ready = []
                ve_ready = []
                for r in range(4):
                    tK = None
                    for cc in range(8):
                        tK = S.add("T", lambda t, r=r, cc=cc, sweep=sweep: t.matmul(
                            KPP[:, :], WA[:, cc, sweep * 4 + r, :],
                            XNF[:, cc, :, :].rearrange("p s n -> p (s n)"),
                            start=(cc == 0), stop=(cc == 7)),
                            waits=(xnf_parts + kpp_free + [kf_free, ("W", wWK)])
                                  if cc == 0 else [], inc="T" if cc == 7 else None)
                    vK = S.add("V", lambda v, r=r, sweep=sweep: v.tensor_scalar(
                        KF[:, r, :], KPP[:, :],
                        BKc[:, sweep * 4 + r:sweep * 4 + r + 1], None, ALU.add),
                        waits=[("T", tK)], inc="V")
                    kpp_free = [("V", vK)]
                    kf_ready.append(("V", vK))

                    tV = None
                    for cc in range(8):
                        tV = S.add("T", lambda t, r=r, cc=cc, sweep=sweep: t.matmul(
                            VPP[:, :], XNF[:, cc, r, :],
                            WB[:, cc, sweep * 512:(sweep + 1) * 512],
                            start=(cc == 0), stop=False),
                            waits=(vpp_free + [ve_free, ("W", wWV)]) if cc == 0 else [],
                            inc=None)
                    tV = S.add("T", lambda t, sweep=sweep: t.matmul(
                        VPP[:, :], ONE1[:, :], BVr[:, sweep * 512:(sweep + 1) * 512],
                        start=False, stop=True), inc="T")
                    vV = S.add("V", lambda v, r=r: v.tensor_copy(
                        VE[:, r, :, 0:64], VPP[:, :].rearrange("p (h d) -> p h d", h=8)),
                        waits=[("T", tV), ("G", gVE1)], inc="V")
                    vpp_free = [("V", vV)]
                    ve_ready.append(("V", vV))

                KPBv = KPB.rearrange("p (s h m) -> p s h m", s=4, h=8)
                kpb_ready = []
                for sub in range(4):
                    for hp in range(4):
                        zslot = (sub * 4 + hp) % 2
                        zbank = ZKP if zslot == 0 else KPP
                        extra = kpp_free if zslot == 1 else []
                        tZ = S.add("T", lambda t, sub=sub, hp=hp, sweep=sweep, zbank=zbank:
                                   t.matmul(zbank[:, :], KF[:, hp, sub * 128:(sub + 1) * 128],
                                            PD[:, sweep * 4 + hp, :], start=True, stop=True),
                                   waits=[kf_ready[hp]] + zk_free[zslot] + extra, inc="T")
                        vM = S.add("V", lambda v, zslot=zslot, zbank=zbank: v.reduce_max(
                            MXZ[:, zslot * 2:zslot * 2 + 2],
                            zbank[:, :].rearrange("p (h m) -> p h m", h=2),
                            axis=AX.X, negate=True),
                            waits=[("T", tZ)], inc="V")
                        sE = None
                        for lo in range(2):
                            sE = S.add("S", lambda s, lo=lo, zslot=zslot, zbank=zbank:
                                       s.activation(KPF[:, zslot, lo * 256:(lo + 1) * 256],
                                                    zbank[:, lo * 256:(lo + 1) * 256],
                                                    AF.Exp, bias=MXZ[:, zslot * 2 + lo:zslot * 2 + lo + 1],
                                                    scale=1.0),
                                       waits=[("V", vM), kpf_free[zslot]] if lo == 0
                                             else [("V", vM)], inc="S")
                        zk_free[zslot] = [("V", vM), ("S", sE)]
                        if zslot == 1:
                            kpp_free = [("V", vM), ("S", sE)]
                        gE = S.add("G", lambda g, sub=sub, hp=hp, zslot=zslot: g.tensor_scalar(
                            KPBv[:, sub, hp * 2:(hp + 1) * 2, :].rearrange("p h m -> p (h m)"),
                            KPF[:, zslot, :], INV16, FM_EPS16, ALU.mult, ALU.add),
                            waits=[("S", sE), kpb_free], inc="G")
                        kpf_free[zslot] = ("G", gE)
                        kpb_ready.append(("G", gE))

                for sub in range(4):
                    for h in range(8):
                        kvt = kv_of[h // 4]
                        kvv = kvt.rearrange("p (q m) -> p q m", q=4)
                        hq = h % 4
                        bank_key = (h // 4, hq // 2)
                        st = bank_key not in kv_started
                        kv_started.add(bank_key)
                        last = (bt == NB1 - 1 and sub == 3 and (hq % 2 == 1))
                        last_kv = S.add("T", lambda t, sub=sub, h=h, kvv=kvv, hq=hq,
                                        st=st, last=last: t.matmul(
                            kvv[0:65, hq, :], VE[:, sub, h, 0:65], KPBv[:, sub, h, :],
                            start=st, stop=last),
                            waits=[kpb_ready[sub * 4 + h // 2], ve_ready[sub],
                                   ("V", sweep_evac) if st else None],
                            inc="T" if (sub == 3 and h == 7) else None) or last_kv
                kf_free = ("T", S.cnt["T"])
                ve_free = ("T", S.cnt["T"])
                kpb_free = ("T", S.cnt["T"])
                xnf_free = ("T", S.cnt["T"])

            KVSv = KVS.rearrange("p (h m) -> p h m", h=16)
            vEv = None
            for h in range(8):
                kvt = kv_of[h // 4]
                kvv = kvt.rearrange("p (q m) -> p q m", q=4)
                vEv = S.add("V", lambda v, h=h, kvv=kvv, sweep=sweep: v.tensor_copy(
                    KVSv[0:65, sweep * 8 + h, :], kvv[0:65, h % 4, :]),
                    waits=[("T", last_kv)] if h == 0 else [], inc="V")
            sweep_evac = vEv

        KVSv = KVS.rearrange("p (h m) -> p h m", h=16)
        vKb = ("V", 0)
        for h in range(H):
            for mc in range(2):
                tKt = S.add("T", lambda t, h=h, mc=mc: t.transpose(
                    KPP[:, 0:65], KVSv[0:65, h, mc * 128:(mc + 1) * 128], IDF[0:65, 0:65]),
                    waits=[("V", sweep_evac), vKb] + kpp_free, inc="T")
                vKb = ("V", S.add("V", lambda v, h=h, mc=mc: v.tensor_copy(
                    KVB[:, mc, h, 0:65], KPP[:, 0:65]),
                    waits=[("T", tKt)], inc="V"))
                kpp_free = []
        kvb_ready = vKb
        kpp_free = [vKb]

        ph1_T_done = ("T", S.cnt["T"])
        wWQ = None
        for cc in range(8):
            wWQ = S.add("G", lambda g, cc=cc: g.dma_start(
                out=WA[:, cc, :, :].rearrange("p a b -> p (a b)"),
                in_=wqkv_d[cc * 128:(cc + 1) * 128, 0:1024]),
                waits=[ph1_T_done] if cc == 0 else [], inc="W")
        wWO = None
        for cc in range(8):
            wWO = S.add("G", lambda g, cc=cc: g.dma_start(
                out=WB[:, cc, :], in_=wout_d[cc * 128:(cc + 1) * 128, :]), inc="W")
        wBO = S.add("G", lambda g: g.dma_start(out=BOr[:, :], in_=bout_d[None, :]), inc="W")

        QPB = KPB.rearrange("p (h m) -> p h m", h=32)
        QFM = KPB.rearrange("p (g n) -> p g n", g=64)
        SEL = KVS.rearrange("p (s f) -> p s f", s=4)
        OTM = VE.rearrange("p s h e -> p (s h) e")
        OFMb = KF.rearrange("p a b -> p (a b)")[:, 0:1024].rearrange("p (g n) -> p g n", g=8)

        xnf_free = ph1_T_done
        qf_free = ("T", 0)
        qpb_free = ph1_T_done
        qfm_free = ("T", 0)
        otm_free = ("T", 0)
        ofm_free = ("T", 0)
        zq_free = zk_free
        qpf_free = dict(kpf_free)
        num_free = {0: list(vpp_free), 1: [("V", 0)]}
        opp_free = {0: [("V", sweep_evac)], 1: [("V", sweep_evac)]}
        zep_free = {0: [("V", sweep_evac)], 1: [("V", sweep_evac)]}
        wet_free = {0: ("T", 0), 1: ("T", 0)}
        x1_free = ("O", 0)
        sel_free = ("O", 0)
        yn_free = ("G", 0)
        stmp_free = {0: ("G", 0), 1: ("G", 0)}
        OPPf = KV0f
        ZEPf = KV1f

        for bt in range(NB2):
            xnf_parts = []
            xw_subs = []
            for sub in range(4):
                ts = bt * 4 + sub
                slot = sub
                xw = S.add("Y", lambda y, ts=ts, slot=slot: y.dma_start(
                    out=XT[:, slot, :], in_=x_d[ts * 128:(ts + 1) * 128, :]),
                    waits=xt_free[slot], inc="X")
                ns = sub % 2
                sN = S.add("S", lambda s, slot=slot, ns=ns, ts=ts: s.activation(
                    XNB[:, ns, :], XT[:, slot, :], AF.Identity,
                    bias=STn[:, ts:ts + 1], scale=STs[:, ts:ts + 1]),
                    waits=[("X", xw), xnb_free[ns]], inc="S")
                xw_subs.append((slot, xw, sN))
                dT = None
                for ccg in range(8):
                    dT = S.add("Y", lambda y, ns=ns, ccg=ccg, sub=sub: y.dma_start_transpose(
                        out=XNF[:, ccg, sub, :],
                        in_=XNB[:, ns, ccg * 128:(ccg + 1) * 128]),
                        waits=[("S", sN), xnf_free] if ccg == 0 else [],
                        inc="D")
                xnb_free[ns] = ("D", dT)
                xnf_parts.append(("D", dT))

            qf_parts = []
            for jc in range(8):
                tQ = None
                for cc in range(8):
                    tQ = S.add("T", lambda t, jc=jc, cc=cc: t.matmul(
                        KPP[:, :], WA[:, cc, jc, :],
                        XNF[:, cc, :, :].rearrange("p s n -> p (s n)"),
                        start=(cc == 0), stop=(cc == 7)),
                        waits=(xnf_parts + kpp_free + [("W", wWQ), qf_free, ("P", pBQ)])
                              if cc == 0 else [], inc="T" if cc == 7 else None)
                vQ = S.add("V", lambda v, jc=jc: v.tensor_scalar(
                    QF[:, jc, :], KPP[:, :], BQc[:, jc:jc + 1], None, ALU.add),
                    waits=[("T", tQ)], inc="V")
                kpp_free = [("V", vQ)]
                qf_parts.append(("V", vQ))

            for sub in range(4):
                slot, xw, sN_sub = xw_subs[sub]
                qpb_ready = []
                for hp in range(8):
                    zslot = hp % 2
                    zbank = ZKP if zslot == 0 else KPP
                    extra = kpp_free if zslot == 1 else []
                    tZ = S.add("T", lambda t, hp=hp, sub=sub, zbank=zbank: t.matmul(
                        zbank[:, :], QF[:, hp, sub * 128:(sub + 1) * 128], PD[:, hp, :],
                        start=True, stop=True),
                        waits=[qf_parts[hp], qpb_free] + zq_free[zslot] + extra, inc="T")
                    vM = S.add("V", lambda v, zslot=zslot, zbank=zbank: v.reduce_max(
                        MXZ[:, zslot * 2:zslot * 2 + 2],
                        zbank[:, :].rearrange("p (h m) -> p h m", h=2),
                        axis=AX.X, negate=True),
                        waits=[("T", tZ)], inc="V")
                    sE = None
                    for lo in range(2):
                        sE = S.add("S", lambda s, lo=lo, zslot=zslot, zbank=zbank:
                                   s.activation(KPF[:, zslot, lo * 256:(lo + 1) * 256],
                                                zbank[:, lo * 256:(lo + 1) * 256], AF.Exp,
                                                bias=MXZ[:, zslot * 2 + lo:zslot * 2 + lo + 1],
                                                scale=1.0),
                                   waits=[("V", vM), kpf_free[zslot]] if lo == 0
                                         else [("V", vM)], inc="S")
                    zq_free[zslot] = [("V", vM), ("S", sE)]
                    if zslot == 1:
                        kpp_free = [("V", vM), ("S", sE)]
                    gE = S.add("G", lambda g, hp=hp, zslot=zslot: g.tensor_scalar(
                        QPB[:, hp * 2:(hp + 1) * 2, :].rearrange("p h m -> p (h m)"),
                        KPF[:, zslot, :], INV16, FM_EPS16, ALU.mult, ALU.add),
                        waits=[("S", sE), qpb_free], inc="G")
                    qpf_free[zslot] = ("G", gE)
                    qpb_ready.append(("G", gE))

                qfm_ready = []
                for rnd in range(4):
                    dQt = None
                    for k in range(8):
                        idx = rnd * 8 + k
                        h, mc = idx // 2, idx % 2
                        dQt = S.add("Y", lambda y, h=h, mc=mc: y.dma_start_transpose(
                            out=QFM[:, 32 + h * 2 + mc, :],
                            in_=QPB[:, h, mc * 128:(mc + 1) * 128]),
                            waits=[qpb_ready[h // 2]] + ([qfm_free] if idx == 0 else []),
                            inc="D")
                    qfm_ready.append(("D", dQt))
                qpb_free_next = ("D", dQt)

                otm_ready = []
                for rnd in range(4):
                    nslot = rnd % 2
                    nbank = VPP if nslot == 0 else KPP
                    nview = nbank[:, 0:260].rearrange("p (h m) -> p h m", h=4)
                    extra = kpp_free if nslot == 1 else []
                    tN = None
                    for k in range(4):
                        h = rnd * 4 + k
                        for mc in range(2):
                            tN = S.add("T", lambda t, h=h, mc=mc, k=k, nview=nview: t.matmul(
                                nview[:, k, :], QFM[:, 32 + h * 2 + mc, :],
                                KVB[:, mc, h, 0:65],
                                start=(k == 0 and mc == 0), stop=(k == 3 and mc == 1)),
                                waits=([qfm_ready[rnd], kvb_ready, otm_free] +
                                       num_free[nslot] + extra)
                                      if (k == 0 and mc == 0) else [],
                                inc="T" if (k == 3 and mc == 1) else None)
                    vD = S.add("V", lambda v, rnd=rnd, nview=nview: v.tensor_scalar(
                        DEN[:, rnd * 4:(rnd + 1) * 4], nview[:, :, 64:65],
                        DEN_EPS, None, ALU.add),
                        waits=[("T", tN)], inc="V")
                    vR = S.add("V", lambda v, rnd=rnd: v.reciprocal(
                        RDEN[:, rnd * 4:(rnd + 1) * 4], DEN[:, rnd * 4:(rnd + 1) * 4]),
                        waits=rt("V"), inc="V")
                    vO = None
                    for k in range(4):
                        h = rnd * 4 + k
                        vO = S.add("V", lambda v, h=h, k=k, nview=nview: v.tensor_scalar(
                            OTMB[:, h, :], nview[:, k, 0:64], RDEN[:, h:h + 1],
                            None, ALU.mult),
                            waits=rt("V") if k == 0 else [], inc="V")
                    num_free[nslot] = [("V", vO)]
                    if nslot == 1:
                        kpp_free = [("V", vO)]
                    otm_ready.append(("V", vO))
                qfm_free = ("T", S.cnt["T"])

                dOt = None
                for ccg in range(8):
                    dOt = S.add("Y", lambda y, ccg=ccg: y.dma_start_transpose(
                        out=OFMb[:, ccg, :],
                        in_=OTMB[:, 2 * ccg:2 * ccg + 2, :].rearrange("p h e -> p (h e)")),
                        waits=[otm_ready[ccg // 2]] + ([ofm_free] if ccg == 0 else []),
                        inc="D")
                vOf = ("D", dOt)
                otm_free = ("D", dOt)

                tO = None
                for cc in range(8):
                    for hh in range(2):
                        tO = S.add("T", lambda t, cc=cc, hh=hh: t.matmul(
                            OPPf[:, hh * 512:(hh + 1) * 512], OFMb[:, cc, :],
                            WB[:, cc, hh * 512:(hh + 1) * 512],
                            start=(cc == 0), stop=False),
                            waits=([vOf, ("W", wBO)] + opp_free[0] + opp_free[1])
                                  if (cc == 0 and hh == 0) else [], inc=None)
                for hh in range(2):
                    tO = S.add("T", lambda t, hh=hh: t.matmul(
                        OPPf[:, hh * 512:(hh + 1) * 512], ONE1[:, :],
                        BOr[:, hh * 512:(hh + 1) * 512], start=False, stop=True),
                        inc="T" if hh == 1 else None)
                ofm_free = ("T", S.cnt["T"])

                ln_bnstats(sub, OPPf[:, 0:512], OPPf[:, 512:1024], waits=[("T", tO)])
                ln_aggr(sub)
                vn = ln_combine(sub, 1)
                sY = S.add("S", lambda s, sub=sub: s.activation(
                    YN[:, :], OPPf[:, :], AF.Identity,
                    bias=LNm[:, sub:sub + 1], scale=LSc[:, sub:sub + 1]),
                    waits=[("V", vn), yn_free], inc="S")
                opp_free[0] = [("S", sY)]
                opp_free[1] = [("S", sY)]
                gX1 = S.add("G", lambda g, sub=sub, slot=slot: g.tensor_tensor(
                    X1B[:, sub, :], XT[:, slot, :], YN[:, :], ALU.add),
                    waits=[("S", sY), ("X", xw), x1_free], inc="G")
                yn_free = ("G", gX1)
                xt_free[slot] = [("G", gX1), ("S", sN_sub)]
                qpb_free = qpb_free_next
            qf_free = ("T", S.cnt["T"])

            gX1_last = ("G", S.cnt["G"])

            for sub in range(4):
                ln_bnstats(sub, X1B[:, sub, 0:512], X1B[:, sub, 512:1024],
                           waits=[gX1_last] if sub == 0 else [])
            for sub in range(4):
                ln_aggr(sub)
            vn2 = ln_combine(0, 4)
            xn2_parts = []
            xnf2_parts = []
            for sub in range(4):
                slot = sub % 2
                sN2 = S.add("S", lambda s, sub=sub: s.activation(
                    XN2[:, sub, :], X1B[:, sub, :], AF.Identity,
                    bias=LNm[:, sub:sub + 1], scale=LSc[:, sub:sub + 1]),
                    waits=[("V", vn2), sel_free], inc="S")
                xn2_parts.append(("S", sN2))
                sC = S.add("S", lambda s, sub=sub, slot=slot: s.activation(
                    XNB[:, slot, :], X1B[:, sub, :], AF.Identity,
                    bias=LNm[:, sub:sub + 1], scale=LSc[:, sub:sub + 1]),
                    waits=[xnb_free[slot]], inc="S")
                dT = None
                for ccg in range(8):
                    dT = S.add("Y", lambda y, slot=slot, ccg=ccg, sub=sub: y.dma_start_transpose(
                        out=XNF[:, ccg, sub, :],
                        in_=XNB[:, slot, ccg * 128:(ccg + 1) * 128]),
                        waits=[("S", sC), xnf_free] if ccg == 0 else [],
                        inc="D")
                xnb_free[slot] = ("D", dT)
                xnf2_parts.append(("D", dT))

            gate_groups = []
            for sub in range(4):
                tGt = None
                for cc in range(8):
                    tGt = S.add("T", lambda t, sub=sub, cc=cc: t.matmul(
                        KPP[:, sub * 8:(sub + 1) * 8], XNF[:, cc, sub, :], GT[:, cc, :],
                        start=(cc == 0), stop=(cc == 7)),
                        waits=([xnf2_parts[sub], ("V", vGT)] + kpp_free)
                              if cc == 0 else [], inc="T" if cc == 7 else None)
                gate_groups.append(tGt)
            for sub in range(4):
                S.add("V", lambda v, sub=sub: v.max(
                    GMX[:, sub, :], KPP[:, sub * 8:(sub + 1) * 8]),
                    waits=[("T", gate_groups[sub])], inc="V")
            vg = S.cnt["V"]
            for sub in range(4):
                S.add("V", lambda v, sub=sub: v.max_index(
                    GIX[:, sub, :], GMX[:, sub, :], KPP[:, sub * 8:(sub + 1) * 8]),
                    waits=[("V", vg)] if sub == 0 else [], inc="V")
            vg = S.cnt["V"]
            for sub in range(4):
                S.add("V", lambda v, sub=sub: v.tensor_copy(
                    IXF[:, sub:sub + 1], GIX[:, sub, 0:1]),
                    waits=[("V", vg)] if sub == 0 else [], inc="V")
            vg = S.cnt["V"]
            vOH = None
            for sub in range(4):
                vOH = S.add("V", lambda v, sub=sub: v.tensor_scalar(
                    OH[:, sub, :], IO8[:, :], IXF[:, sub:sub + 1], None, ALU.is_equal),
                    waits=[("V", vg), ("G", gIot)] if sub == 0 else [], inc="V")
            kpp_free = [("V", vOH)]

            last_sel = {}
            for e in range(NE):
                for fh in range(2):
                    wslot = (e * 2 + fh) % 2
                    wE = S.add("Y", lambda y, e=e, fh=fh, wslot=wslot: y.dma_start(
                        out=WET[:, wslot, :, :],
                        in_=wet_d[e, :, fh * 512:(fh + 1) * 512].rearrange(
                            "(hc p) f -> p hc f", p=128)),
                        waits=[wet_free[wslot], ("P", yWET_done)], inc="E")
                    for sub in range(4):
                        sidx = (fh * 4 + sub) % 4
                        zt, zhalf = (ZEPf, sidx % 2) if sidx < 2 else (OPPf, sidx % 2)
                        bankfree = zep_free[zhalf] if sidx < 2 else opp_free[zhalf]
                        tZe = None
                        for cc in range(8):
                            tZe = S.add("T", lambda t, cc=cc, sub=sub, wslot=wslot,
                                        zt=zt, zhalf=zhalf: t.matmul(
                                zt[:, zhalf * 512:(zhalf + 1) * 512],
                                XNF[:, cc, sub, :], WET[:, wslot, cc, :],
                                start=(cc == 0), stop=(cc == 7)),
                                waits=([("E", wE), xnf2_parts[sub], ("V", vOH)] + bankfree)
                                      if cc == 0 else [], inc="T" if cc == 7 else None)
                        selw = [("T", tZe)]
                        if e > 0:
                            selw.append(last_sel[(sub, fh)])
                        else:
                            selw.append(xn2_parts[sub])
                        src1 = (XN2[:, sub, fh * 512:(fh + 1) * 512] if e == 0
                                else SEL[:, sub, fh * 512:(fh + 1) * 512])
                        dst = SEL[:, sub, fh * 512:(fh + 1) * 512]
                        zsrc = zt[:, zhalf * 512:(zhalf + 1) * 512]
                        ohap = OH[:, sub, e:e + 1]
                        if sidx % 2 == 0:
                            sv = S.add("V", lambda v, dst=dst, zsrc=zsrc, ohap=ohap,
                                       src1=src1: v.scalar_tensor_tensor(
                                           dst, zsrc, ohap, src1, ALU.mult, ALU.add),
                                       waits=selw, inc="V")
                            tok = ("V", sv)
                            bank_tok = tok
                        else:
                            pslot = sub % 2
                            s1 = S.add("S", lambda s, zsrc=zsrc, ohap=ohap, pslot=pslot:
                                       s.activation(STMP[:, pslot, :], zsrc, AF.Identity,
                                                    bias=0.0, scale=ohap),
                                       waits=[("T", tZe), stmp_free[pslot]], inc="S")
                            g1 = S.add("G", lambda g, dst=dst, src1=src1, pslot=pslot:
                                       g.tensor_tensor(dst, src1, STMP[:, pslot, :],
                                                       ALU.add),
                                       waits=[("S", s1)] + selw[1:], inc="G")
                            stmp_free[pslot] = ("G", g1)
                            tok = ("G", g1)
                            bank_tok = ("S", s1)
                        last_sel[(sub, fh)] = tok
                        if sidx < 2:
                            zep_free[zhalf] = [bank_tok]
                        else:
                            opp_free[zhalf] = [bank_tok]
                    wet_free[wslot] = ("T", S.cnt["T"])

            for sub in range(4):
                ln_bnstats(sub, SEL[:, sub, 0:512], SEL[:, sub, 512:1024],
                           waits=[last_sel[(sub, 0)], last_sel[(sub, 1)]])
            for sub in range(4):
                ln_aggr(sub)
            vn3 = ln_combine(0, 4)
            gOut = None
            for sub in range(4):
                sY2 = S.add("S", lambda s, sub=sub: s.activation(
                    YN[:, :], SEL[:, sub, :], AF.Identity,
                    bias=LNm[:, sub:sub + 1], scale=LSc[:, sub:sub + 1]),
                    waits=[("V", vn3), yn_free], inc="S")
                gOut = S.add("G", lambda g, sub=sub: g.tensor_tensor(
                    X1B[:, sub, :], X1B[:, sub, :], YN[:, :], ALU.add),
                    waits=[("S", sY2)], inc="G")
                yn_free = ("G", gOut)
            yO = S.add("Y", lambda y, bt=bt: y.dma_start(
                out=out_d[bt * 512:(bt + 1) * 512, :].rearrange("(s p) c -> p s c", p=128),
                in_=X1B[:, :, :]),
                waits=[("G", gOut)], inc="O")
            x1_free = ("O", yO)
            sel_free = ("O", yO)
            xnf_free = ("T", S.cnt["T"])

        if dbg:
            allw = [("T", S.cnt["T"]), ("V", S.cnt["V"]), ("S", S.cnt["S"]),
                    ("G", S.cnt["G"])]
            dbg_list = [
                ("d_STs", STs, F32), ("d_STn", STn, F32),
                ("d_KVB", KVB, BF16), ("d_QF", QF, BF16),
                ("d_KPB", KPB, BF16), ("d_OTMB", OTMB, BF16),
                ("d_XN2", XN2, F32), ("d_OH", OH, F32),
                ("d_SEL", KVS, F32), ("d_DEN", DEN, F32),
                ("d_XNF", XNF, BF16), ("d_X1B", X1B, F32),
                ("d_PD", PD, BF16), ("d_GT", GT, BF16),
                ("d_VE", VE, BF16), ("d_KF", KF, BF16),
                ("d_BQc", BQc, F32), ("d_BKc", BKc, F32),
                ("d_RDEN", RDEN, F32), ("d_MV", MV, F32),
                ("d_LSc", LSc, F32), ("d_QFb", QF, BF16),
            ]
            for nm, buf, dt in dbg_list:
                shp = [128] + [int(s) for s in buf.shape[1:]]
                dd = nc.dram_tensor(nm, shp, dt, kind="ExternalOutput").ap()
                S.add("Y", lambda y, dd=dd, buf=buf: y.dma_start(out=dd[:], in_=buf[:]),
                      waits=allw, inc="O")
        S.add("Y", lambda y: y.nop(), waits=[("O", S.cnt["O"])])

        with nc.Block() as block:
            @block.tensor
            def _(t):
                S.emit(t, "T", sems)

            @block.vector
            def _(v):
                S.emit(v, "V", sems)

            @block.scalar
            def _(s):
                S.emit(s, "S", sems)

            @block.gpsimd
            def _(g):
                S.emit(g, "G", sems)

            @block.sync
            def _(y):
                S.emit(y, "Y", sems)

    return nc


def kernel(L0_attn_x, L0_attn_w_qkv, L0_attn_b_qkv, L0_attn_w_out, L0_attn_b_out,
           L0_attn_proj_matrix, L0_ffn_gate_w, L0_ffn_expert_w):
    x = np.ascontiguousarray(np.asarray(L0_attn_x, dtype=np.float32))
    wqkv = np.ascontiguousarray(np.asarray(L0_attn_w_qkv, dtype=np.float32))
    bqkv = np.ascontiguousarray(np.asarray(L0_attn_b_qkv, dtype=np.float32))
    wout = np.ascontiguousarray(np.asarray(L0_attn_w_out, dtype=np.float32))
    bout = np.ascontiguousarray(np.asarray(L0_attn_b_out, dtype=np.float32))
    proj = np.ascontiguousarray(np.asarray(L0_attn_proj_matrix, dtype=np.float32))
    gate = np.ascontiguousarray(np.asarray(L0_ffn_gate_w, dtype=np.float32))
    expw = np.ascontiguousarray(np.asarray(L0_ffn_expert_w, dtype=np.float32))

    nc = build_program()
    in_maps = []
    for c in range(8):
        b, hf = c // 2, c % 2
        own = x[b, hf * 4096:(hf + 1) * 4096]
        other = x[b, (1 - hf) * 4096:(2 - hf) * 4096]
        xp = np.ascontiguousarray(np.concatenate([own, other], axis=0))
        in_maps.append({
            "x_seq": xp, "w_qkv": wqkv, "b_qkv": bqkv, "w_out": wout,
            "b_out": bout, "proj": proj, "gate_w": gate, "expert_w": expw,
        })
    res = run_bass_kernel_spmd(nc, in_maps, core_ids=list(range(8)))
    out = np.empty((4, 8192, C), dtype=np.float32)
    for c in range(8):
        b, hf = c // 2, c % 2
        out[b, hf * 4096:(hf + 1) * 4096] = res.results[c]["out"]
    return out


# version 05
# speedup vs baseline: 1.8000x; 1.8000x over previous
import numpy as np
from contextlib import ExitStack

import concourse.bass as bass
import concourse.mybir as mybir
from concourse.bass_utils import run_bass_kernel_spmd

F32 = mybir.dt.float32
BF16 = mybir.dt.bfloat16
U32 = mybir.dt.uint32
AF = mybir.ActivationFunctionType
ALU = mybir.AluOpType
AX = mybir.AxisListType

C = 1024
H = 16
E = 8
LN_EPS = 1e-5
FM_EPS16 = 1e-6 / 16.0
DEN_EPS = 1e-6
DN = float(64.0 ** (-0.25))
INV16 = 1.0 / 16.0

DMA_SEMS = frozenset("XWEOP")
SEM_NAMES = ["T", "V", "S", "G", "X", "W", "E", "O", "P"]


class Sched:
    def __init__(self):
        self.cnt = {s: 0 for s in SEM_NAMES}
        self.ops = {e: [] for e in "TVSGY"}

    def add(self, eng, fn, waits=(), inc=None, amt=1):
        ws = []
        for w in waits:
            if w is None:
                continue
            s, v = w
            if v and v > 0:
                ws.append((s, v))
        a = 0
        val = None
        if inc is not None:
            a = 16 if inc in DMA_SEMS else amt
            self.cnt[inc] += a
            val = self.cnt[inc]
        self.ops[eng].append((fn, ws, inc, a))
        return val

    def emit(self, eng_obj, eng_name, sems):
        maxw = {}
        for fn, ws, inc, a in self.ops[eng_name]:
            for s, v in ws:
                if s == eng_name and s == "T":
                    continue
                if maxw.get(s, 0) >= v:
                    continue
                maxw[s] = v
                eng_obj.wait_ge(sems[s], v)
            instr = fn(eng_obj)
            if inc is not None:
                instr.then_inc(sems[inc], a)


def build_program(NB1=16, NB2=8, NE=8, dbg=False):
    nc = bass.Bass("TRN2", target_bir_lowering=False)

    x_d = nc.dram_tensor("x_seq", [8192, C], F32, kind="ExternalInput").ap()
    wqkv_d = nc.dram_tensor("w_qkv", [C, 3 * C], F32, kind="ExternalInput").ap()
    bqkv_d = nc.dram_tensor("b_qkv", [3 * C], F32, kind="ExternalInput").ap()
    wout_d = nc.dram_tensor("w_out", [C, C], F32, kind="ExternalInput").ap()
    bout_d = nc.dram_tensor("b_out", [C], F32, kind="ExternalInput").ap()
    proj_d = nc.dram_tensor("proj", [H, 256, 64], F32, kind="ExternalInput").ap()
    gate_d = nc.dram_tensor("gate_w", [E, C], F32, kind="ExternalInput").ap()
    expw_d = nc.dram_tensor("expert_w", [E, C, C], F32, kind="ExternalInput").ap()
    out_d = nc.dram_tensor("out", [4096, C], F32, kind="ExternalOutput").ap()
    wet_d = nc.dram_tensor("wet_scratch", [E, C, C], BF16, kind="Internal").ap()
    xnf_d = nc.dram_tensor("xnf_scr", [16, 128, 4096], BF16, kind="Internal").ap()

    es = ExitStack()
    S = Sched()

    def sb(name, shape, dt=F32):
        return es.enter_context(nc.sbuf_tensor(name, shape, dt))

    def psum(name, shape, dt=F32):
        return es.enter_context(nc.psum_tensor(name, shape, dt))

    with es:
        WA = sb("WA", [128, 8, 8, 128], BF16)
        WB = sb("WB", [128, 8, C], BF16)
        PD = sb("PD", [128, 8, 512], BF16)
        GT = sb("GT", [128, 8, 8], BF16)
        BQc = sb("BQc", [128, 8], F32)
        BKc = sb("BKc", [128, 8], F32)
        BVr = sb("BVr", [1, C], BF16)
        BOr = sb("BOr", [1, C], BF16)
        ONE1 = sb("ONE1", [1, 128], BF16)
        IDB = sb("IDB", [128, 128], BF16)
        IDF = sb("IDF", [128, 128], F32)
        IO8 = sb("IO8", [128, 8], F32)
        STs = sb("STs", [128, 64], F32)
        STn = sb("STn", [128, 64], F32)
        KVB = sb("KVB", [128, 2, H, 68], BF16)
        GW8 = sb("GW8", [8, C], BF16)

        XT = sb("XT", [128, 4, C], F32)
        XNB = sb("XNB", [128, 2, C], BF16)
        XNF = sb("XNF", [128, 8, 4, 128], BF16)
        XNF2 = sb("XNF2", [128, 8, 4, 128], BF16)
        YN2 = sb("YN2", [128, C], F32)
        KF = sb("KF", [128, 4, 512], BF16)
        VE = sb("VE", [128, 4, 8, 68], BF16)
        KPF = sb("KPF", [128, 2, 512], F32)
        KPB = sb("KPB", [128, 8192], BF16)
        KVS = sb("KVS", [128, 4096], F32)
        QF = sb("QF", [128, 8, 512], BF16)
        X1Ba = sb("X1Ba", [128, 4, C], F32)
        X1Bb = sb("X1Bb", [128, 4, C], F32)
        YN = sb("YN", [128, C], F32)
        OH = sb("OH", [128, 4, 8], F32)
        WET = sb("WET", [128, 2, 8, 512], BF16)
        OTMB = sb("OTMB", [128, 16, 64], BF16)
        DEN = sb("DEN", [128, 16], F32)
        RDEN = sb("RDEN", [128, 16], F32)
        MXZ = sb("MXZ", [128, 4], F32)
        GMX = sb("GMX", [128, 4, 8], F32)
        GIX = sb("GIX", [128, 4, 8], U32)
        IXF = sb("IXF", [128, 4], F32)

        STMP = sb("STMP", [128, 2, 512], F32)
        BNS = sb("BNS", [128, 4, 12], F32)
        MV = sb("MV", [128, 4, 2, 2], F32)
        LT0 = sb("LT0", [128, 4], F32)
        LT1 = sb("LT1", [128, 4], F32)
        LT2 = sb("LT2", [128, 4], F32)
        LT3 = sb("LT3", [128, 4], F32)
        LSc = sb("LSc", [128, 4], F32)
        LNm = sb("LNm", [128, 4], F32)

        TRB = psum("TRB", [128, 1024], BF16)
        KPP = psum("KPP", [128, 512], F32)
        VPP = psum("VPP", [128, 512], F32)
        ZKP = psum("ZKP", [128, 512], F32)
        KV0f = psum("KV0f", [128, 1024], F32)
        KV1f = psum("KV1f", [128, 1024], F32)
        kv_of = {0: KV0f, 1: KV1f}

        sems = {}
        for name in SEM_NAMES:
            sems[name] = es.enter_context(nc.semaphore("sem" + name))

        def rt(eng):
            return [(eng, S.cnt[eng])]

        def ln_bnstats(col, src0, src1, waits):
            S.add("V", lambda v, a=src0, c=col: v.bn_stats(BNS[:, c, 0:6], a),
                  waits=waits, inc="V")
            return S.add("V", lambda v, a=src1, c=col: v.bn_stats(BNS[:, c, 6:12], a),
                         inc="V")

        def ln_aggr(col):
            S.add("V", lambda v, c=col: v.bn_aggr(MV[:, c, 0, :], BNS[:, c, 0:6]), inc="V")
            return S.add("V", lambda v, c=col: v.bn_aggr(MV[:, c, 1, :], BNS[:, c, 6:12]),
                         inc="V")

        def ln_combine(c0, n):
            cs = slice(c0, c0 + n)
            m0 = MV[:, cs, 0:1, 0:1]
            m1 = MV[:, cs, 1:2, 0:1]
            v0 = MV[:, cs, 0:1, 1:2]
            v1 = MV[:, cs, 1:2, 1:2]
            S.add("V", lambda v: v.tensor_tensor(LT0[:, cs], m0, m1, ALU.subtract),
                  waits=rt("V"), inc="V")
            S.add("V", lambda v: v.tensor_tensor(LT1[:, cs], v0, v1, ALU.add), inc="V")
            S.add("V", lambda v: v.tensor_tensor(LT3[:, cs], m0, m1, ALU.add), inc="V")
            S.add("V", lambda v: v.tensor_tensor(LT2[:, cs], LT0[:, cs], LT0[:, cs], ALU.mult),
                  waits=rt("V"), inc="V")
            S.add("V", lambda v: v.tensor_scalar(LT1[:, cs], LT1[:, cs], 0.5, LN_EPS,
                                                 ALU.mult, ALU.add), inc="V")
            vv = S.add("V", lambda v: v.scalar_tensor_tensor(LT0[:, cs], LT2[:, cs], 0.25,
                                                             LT1[:, cs], ALU.mult, ALU.add),
                       waits=rt("V"), inc="V")
            sv = S.add("S", lambda s: s.activation(LT2[:, cs], LT0[:, cs], AF.Ln,
                                                   bias=0.0, scale=1.0),
                       waits=[("V", vv)], inc="S")
            sv2 = S.add("S", lambda s: s.activation(LSc[:, cs], LT2[:, cs], AF.Exp,
                                                    bias=0.0, scale=-0.5),
                        waits=[("S", sv)], inc="S")
            S.add("V", lambda v: v.tensor_tensor(LT3[:, cs], LT3[:, cs], LSc[:, cs], ALU.mult),
                  waits=[("S", sv2)], inc="V")
            return S.add("V", lambda v: v.tensor_scalar(LNm[:, cs], LT3[:, cs], -0.5, None,
                                                        ALU.mult),
                         waits=rt("V"), inc="V")

        pBQ = S.add("Y", lambda y: y.dma_start(
            out=BQc[:, :], in_=bqkv_d[0:1024].rearrange("(jc p) -> p jc", p=128),
            allow_slow_non_contiguous=True), inc="P")
        pBK = S.add("Y", lambda y: y.dma_start(
            out=BKc[:, :], in_=bqkv_d[1024:2048].rearrange("(jc p) -> p jc", p=128),
            allow_slow_non_contiguous=True), inc="P")

        wWK = None
        for cc in range(8):
            wWK = S.add("G", lambda g, cc=cc: g.dma_start(
                out=WA[:, cc, :, :].rearrange("p a b -> p (a b)"),
                in_=wqkv_d[cc * 128:(cc + 1) * 128, 1024:2048]), inc="W")
        wWV = None
        for cc in range(8):
            wWV = S.add("G", lambda g, cc=cc: g.dma_start(
                out=WB[:, cc, :], in_=wqkv_d[cc * 128:(cc + 1) * 128, 2048:3072]), inc="W")
        wBV = S.add("G", lambda g: g.dma_start(out=BVr[:, :], in_=bqkv_d[None, 2048:3072]),
                    inc="W")
        wGW = S.add("G", lambda g: g.dma_start(out=GW8[:, :], in_=gate_d[:, :]), inc="W")

        S.add("G", lambda g: g.memset(ONE1[:, :], 1.0), inc="G")
        S.add("G", lambda g: g.iota(XT[:, 0, 0:128], [[1, 128]], base=0, channel_multiplier=0,
                                    allow_small_or_imprecise_dtypes=True), inc="G")
        S.add("G", lambda g: g.iota(XT[:, 0, 128:256], [[0, 128]], base=0, channel_multiplier=1,
                                    allow_small_or_imprecise_dtypes=True), inc="G")
        gIot = S.add("G", lambda g: g.iota(IO8[:, :], [[1, 8]], base=0, channel_multiplier=0,
                                           allow_small_or_imprecise_dtypes=True), inc="G")
        gPD0 = S.add("G", lambda g: g.memset(PD[:, :, :], 0.0), inc="G")
        gVE1 = S.add("G", lambda g: g.memset(VE[:, :, :, 64:68], 1.0), inc="G")

        S.add("V", lambda v: v.tensor_tensor(IDF[:, :], XT[:, 0, 0:128], XT[:, 0, 128:256],
                                             ALU.is_equal),
              waits=[("G", gIot)], inc="V")
        vIDb = S.add("V", lambda v: v.tensor_copy(IDB[:, :], IDF[:, :]),
                     waits=rt("V"), inc="V")

        trb_free = ("V", 0)
        vPD = 0
        for h in range(H):
            wPJ = S.add("G", lambda g, h=h: g.dma_start(
                out=XNB[:, 1, 0:128].rearrange("p (a b) -> p a b", a=2),
                in_=proj_d[h].rearrange("(mc p) d -> p mc d", p=128)),
                waits=[("V", vPD)], inc="W")
            gSc = S.add("G", lambda g: g.tensor_scalar(XNB[:, 1, 0:128], XNB[:, 1, 0:128],
                                                       DN, None, ALU.mult),
                        waits=[("W", wPJ)], inc="G")
            tTR = None
            for mc in range(2):
                tTR = S.add("T", lambda t, mc=mc: t.transpose(
                    TRB[0:64, mc * 128:(mc + 1) * 128],
                    XNB[:, 1, mc * 64:(mc + 1) * 64], IDB[:, :]),
                    waits=[("G", gSc), ("V", vIDb), trb_free] if mc == 0 else [],
                    inc="T" if mc == 1 else None)
            hp, lo = h // 2, h % 2
            vPD = S.add("V", lambda v, hp=hp, lo=lo: v.tensor_copy(
                PD[lo * 64:(lo + 1) * 64, hp, lo * 256:(lo + 1) * 256], TRB[0:64, 0:256]),
                waits=[("T", tTR), ("G", gPD0)], inc="V")
            trb_free = ("V", vPD)

        tG = None
        for cc in range(8):
            tG = S.add("T", lambda t, cc=cc: t.transpose(
                TRB[:, 256 + cc * 8:256 + (cc + 1) * 8],
                GW8[0:8, cc * 128:(cc + 1) * 128], IDB[0:8, 0:8]),
                waits=[("W", wGW), trb_free] if cc == 0 else [],
                inc="T" if cc == 7 else None)
        vGT = S.add("V", lambda v: v.tensor_copy(
            GT[:, :, :].rearrange("p a b -> p (a b)"), TRB[:, 256:320]),
            waits=[("T", tG)], inc="V")
        trb_free = ("V", vGT)

        WETB = KPB.rearrange("p (hc f) -> p hc f", hc=8)
        xnb_free = {0: ("T", 0), 1: ("T", S.cnt["T"])}
        yWET = 0
        for e in range(NE):
            vEv = None
            for fc in range(8):
                slot = fc % 2
                wEW = S.add("G", lambda g, e=e, slot=slot, fc=fc: g.dma_start(
                    out=XNB[:, slot, :], in_=expw_d[e, fc * 128:(fc + 1) * 128, :]),
                    waits=[xnb_free[slot]], inc="W")
                tE = None
                for hc in range(8):
                    tE = S.add("T", lambda t, slot=slot, hc=hc, wEW=wEW: t.transpose(
                        TRB[:, hc * 128:(hc + 1) * 128],
                        XNB[:, slot, hc * 128:(hc + 1) * 128], IDB[:, :]),
                        waits=[("W", wEW), trb_free] if hc == 0 else [],
                        inc="T" if hc == 7 else None)
                xnb_free[slot] = ("T", tE)
                vEv = S.add("V", lambda v, fc=fc: v.tensor_copy(
                    WETB[:, :, fc * 128:(fc + 1) * 128],
                    TRB[:, :].rearrange("p (a b) -> p a b", a=8)),
                    waits=[("T", tE)] + ([("P", yWET)] if fc == 0 else []), inc="V")
                trb_free = ("V", vEv)
            yWET = S.add("Y", lambda y, e=e: y.dma_start(
                out=wet_d[e].rearrange("(hc p) f -> p hc f", p=128), in_=WETB[:, :, :]),
                waits=[("V", vEv)], inc="P")
        yWET_done = S.cnt["P"]
        kpb_free = ("P", yWET_done)

        xt_free = {0: [("V", vIDb)], 1: [("S", 0)], 2: [("S", 0)], 3: [("S", 0)]}
        xnf_free = ("T", 0)
        kf_free = ("T", 0)
        ve_free = ("T", 0)
        kpf_free = {0: ("G", 0), 1: ("G", 0)}
        kpp_free = [("V", 0)]
        vpp_free = [("V", 0)]
        zk_free = {0: [("V", 0), ("S", 0)], 1: [("V", 0), ("S", 0)]}
        sweep_evac = 0

        xnf_store = {}
        for sweep in range(2):
            kv_started = set()
            last_kv = 0
            for bt in range(NB1):
                if sweep == 0:
                    sub_info = []
                    for sub in range(4):
                        ts = bt * 4 + sub
                        slot = sub
                        xw = S.add("Y", lambda y, ts=ts, slot=slot: y.dma_start(
                            out=XT[:, slot, :], in_=x_d[ts * 128:(ts + 1) * 128, :]),
                            waits=xt_free[slot] +
                                  ([("W", wBV), ("P", pBK)] if bt == 0 else []),
                            inc="X")
                        vbn = ln_bnstats(sub, XT[:, slot, 0:512], XT[:, slot, 512:1024],
                                         waits=[("X", xw)])
                        sub_info.append((slot, xw, ts, vbn))
                    for sub in range(4):
                        ln_aggr(sub)
                    ln_combine(0, 4)
                    S.add("V", lambda v, bt=bt: v.tensor_copy(
                        STs[:, bt * 4:(bt + 1) * 4], LSc[:, 0:4]), waits=rt("V"), inc="V")
                    vstat = S.add("V", lambda v, bt=bt: v.tensor_copy(
                        STn[:, bt * 4:(bt + 1) * 4], LNm[:, 0:4]), inc="V")

                    xnf_parts = []
                    for sub in range(4):
                        slot, xw, ts, vbn = sub_info[sub]
                        ns = sub % 2
                        sN = S.add("S", lambda s, slot=slot, ns=ns, ts=ts: s.activation(
                            XNB[:, ns, :], XT[:, slot, :], AF.Identity,
                            bias=STn[:, ts:ts + 1], scale=STs[:, ts:ts + 1]),
                            waits=[("X", xw), ("V", vstat), xnb_free[ns]], inc="S")
                        xt_free[slot] = [("S", sN), ("V", vbn)]
                        tT = None
                        for ccg in range(8):
                            tT = S.add("T", lambda t, ns=ns, ccg=ccg: t.transpose(
                                TRB[:, ccg * 128:(ccg + 1) * 128],
                                XNB[:, ns, ccg * 128:(ccg + 1) * 128], IDB[:, :]),
                                waits=[("S", sN), trb_free, xnf_free] if ccg == 0 else [],
                                inc="T" if ccg == 7 else None)
                        xnb_free[ns] = ("T", tT)
                        vX = S.add("V", lambda v, sub=sub: v.tensor_copy(
                            XNF[:, :, sub, :], TRB[:, :].rearrange("p (a b) -> p a b", a=8)),
                            waits=[("T", tT)], inc="V")
                        trb_free = ("V", vX)
                        xnf_parts.append(("V", vX))
                    xnf_store[bt] = S.add("Y", lambda y, bt=bt: y.dma_start(
                        out=xnf_d[bt],
                        in_=XNF[:, :, :, :].rearrange("p a s n -> p (a s n)")),
                        waits=list(xnf_parts), inc="P")
                else:
                    xfl = S.add("Y", lambda y, bt=bt: y.dma_start(
                        out=XNF[:, :, :, :].rearrange("p a s n -> p (a s n)"),
                        in_=xnf_d[bt]),
                        waits=[xnf_free, ("P", xnf_store[bt])], inc="X")
                    xnf_parts = [("X", xfl)]

                kf_ready = []
                ve_ready = []
                for r in range(4):
                    tK = None
                    for cc in range(8):
                        tK = S.add("T", lambda t, r=r, cc=cc, sweep=sweep: t.matmul(
                            KPP[:, :], WA[:, cc, sweep * 4 + r, :],
                            XNF[:, cc, :, :].rearrange("p s n -> p (s n)"),
                            start=(cc == 0), stop=(cc == 7)),
                            waits=(xnf_parts + kpp_free + [kf_free, ("W", wWK)])
                                  if cc == 0 else [], inc="T" if cc == 7 else None)
                    vK = S.add("V", lambda v, r=r, sweep=sweep: v.tensor_scalar(
                        KF[:, r, :], KPP[:, :],
                        BKc[:, sweep * 4 + r:sweep * 4 + r + 1], None, ALU.add),
                        waits=[("T", tK)], inc="V")
                    kpp_free = [("V", vK)]
                    kf_ready.append(("V", vK))

                    tV = None
                    for cc in range(8):
                        tV = S.add("T", lambda t, r=r, cc=cc, sweep=sweep: t.matmul(
                            VPP[:, :], XNF[:, cc, r, :],
                            WB[:, cc, sweep * 512:(sweep + 1) * 512],
                            start=(cc == 0), stop=False),
                            waits=(vpp_free + [ve_free, ("W", wWV)]) if cc == 0 else [],
                            inc=None)
                    tV = S.add("T", lambda t, sweep=sweep: t.matmul(
                        VPP[:, :], ONE1[:, :], BVr[:, sweep * 512:(sweep + 1) * 512],
                        start=False, stop=True), inc="T")
                    vV = S.add("V", lambda v, r=r: v.tensor_copy(
                        VE[:, r, :, 0:64], VPP[:, :].rearrange("p (h d) -> p h d", h=8)),
                        waits=[("T", tV), ("G", gVE1)], inc="V")
                    vpp_free = [("V", vV)]
                    ve_ready.append(("V", vV))

                KPBv = KPB.rearrange("p (s h m) -> p s h m", s=4, h=8)
                kpb_ready = []
                for sub in range(4):
                    for hp in range(4):
                        zslot = (sub * 4 + hp) % 2
                        zbank = ZKP if zslot == 0 else KPP
                        extra = kpp_free if zslot == 1 else []
                        tZ = S.add("T", lambda t, sub=sub, hp=hp, sweep=sweep, zbank=zbank:
                                   t.matmul(zbank[:, :], KF[:, hp, sub * 128:(sub + 1) * 128],
                                            PD[:, sweep * 4 + hp, :], start=True, stop=True),
                                   waits=[kf_ready[hp]] + zk_free[zslot] + extra, inc="T")
                        vM = S.add("V", lambda v, zslot=zslot, zbank=zbank: v.reduce_max(
                            MXZ[:, zslot * 2:zslot * 2 + 2],
                            zbank[:, :].rearrange("p (h m) -> p h m", h=2),
                            axis=AX.X, negate=True),
                            waits=[("T", tZ)], inc="V")
                        sE = None
                        for lo in range(2):
                            sE = S.add("S", lambda s, lo=lo, zslot=zslot, zbank=zbank:
                                       s.activation(KPF[:, zslot, lo * 256:(lo + 1) * 256],
                                                    zbank[:, lo * 256:(lo + 1) * 256],
                                                    AF.Exp, bias=MXZ[:, zslot * 2 + lo:zslot * 2 + lo + 1],
                                                    scale=1.0),
                                       waits=[("V", vM), kpf_free[zslot]] if lo == 0
                                             else [("V", vM)], inc="S")
                        zk_free[zslot] = [("V", vM), ("S", sE)]
                        if zslot == 1:
                            kpp_free = [("V", vM), ("S", sE)]
                        gE = S.add("G", lambda g, sub=sub, hp=hp, zslot=zslot: g.tensor_scalar(
                            KPBv[:, sub, hp * 2:(hp + 1) * 2, :].rearrange("p h m -> p (h m)"),
                            KPF[:, zslot, :], INV16, FM_EPS16, ALU.mult, ALU.add),
                            waits=[("S", sE), kpb_free], inc="G")
                        kpf_free[zslot] = ("G", gE)
                        kpb_ready.append(("G", gE))

                for sub in range(4):
                    for h in range(8):
                        kvt = kv_of[h // 4]
                        kvv = kvt.rearrange("p (q m) -> p q m", q=4)
                        hq = h % 4
                        bank_key = (h // 4, hq // 2)
                        st = bank_key not in kv_started
                        kv_started.add(bank_key)
                        last = (bt == NB1 - 1 and sub == 3 and (hq % 2 == 1))
                        last_kv = S.add("T", lambda t, sub=sub, h=h, kvv=kvv, hq=hq,
                                        st=st, last=last: t.matmul(
                            kvv[0:65, hq, :], VE[:, sub, h, 0:65], KPBv[:, sub, h, :],
                            start=st, stop=last),
                            waits=[kpb_ready[sub * 4 + h // 2], ve_ready[sub],
                                   ("V", sweep_evac) if st else None],
                            inc="T" if (sub == 3 and h == 7) else None) or last_kv
                kf_free = ("T", S.cnt["T"])
                ve_free = ("T", S.cnt["T"])
                kpb_free = ("T", S.cnt["T"])
                xnf_free = ("T", S.cnt["T"])

            KVSv = KVS.rearrange("p (h m) -> p h m", h=16)
            vEv = None
            for h in range(8):
                kvt = kv_of[h // 4]
                kvv = kvt.rearrange("p (q m) -> p q m", q=4)
                vEv = S.add("V", lambda v, h=h, kvv=kvv, sweep=sweep: v.tensor_copy(
                    KVSv[0:65, sweep * 8 + h, :], kvv[0:65, h % 4, :]),
                    waits=[("T", last_kv)] if h == 0 else [], inc="V")
            sweep_evac = vEv

        KVSv = KVS.rearrange("p (h m) -> p h m", h=16)
        vKb = ("V", 0)
        for h in range(H):
            for mc in range(2):
                tKt = S.add("T", lambda t, h=h, mc=mc: t.transpose(
                    KPP[:, 0:65], KVSv[0:65, h, mc * 128:(mc + 1) * 128], IDF[0:65, 0:65]),
                    waits=[("V", sweep_evac), vKb] + kpp_free, inc="T")
                vKb = ("V", S.add("V", lambda v, h=h, mc=mc: v.tensor_copy(
                    KVB[:, mc, h, 0:65], KPP[:, 0:65]),
                    waits=[("T", tKt)], inc="V"))
                kpp_free = []
        kvb_ready = vKb
        kpp_free = [vKb]

        ph1_T_done = ("T", S.cnt["T"])
        wWQ = None
        for cc in range(8):
            wWQ = S.add("G", lambda g, cc=cc: g.dma_start(
                out=WA[:, cc, :, :].rearrange("p a b -> p (a b)"),
                in_=wqkv_d[cc * 128:(cc + 1) * 128, 0:1024]),
                waits=[ph1_T_done] if cc == 0 else [], inc="W")
        wWO = None
        for cc in range(8):
            wWO = S.add("G", lambda g, cc=cc: g.dma_start(
                out=WB[:, cc, :], in_=wout_d[cc * 128:(cc + 1) * 128, :]), inc="W")
        wBO = S.add("G", lambda g: g.dma_start(out=BOr[:, :], in_=bout_d[None, :]), inc="W")

        QPB = KPB.rearrange("p (h m) -> p h m", h=32)
        QFM = KPB.rearrange("p (g n) -> p g n", g=64)
        SEL = KVS.rearrange("p (s f) -> p s f", s=4)
        OTM = VE.rearrange("p s h e -> p (s h) e")
        OFMb = KF.rearrange("p a b -> p (a b)")[:, 0:1024].rearrange("p (g n) -> p g n", g=8)

        xnf_free = ph1_T_done
        qf_free = ("T", 0)
        qpb_free = ph1_T_done
        qfm_free = ("T", 0)
        otm_free = ("T", 0)
        ofm_free = ("T", 0)
        zq_free = zk_free
        qpf_free = dict(kpf_free)
        num_free = {0: list(vpp_free), 1: [("V", 0)]}
        opp_free = {0: [("V", sweep_evac)], 1: [("V", sweep_evac)]}
        zep_free = {0: [("V", sweep_evac)], 1: [("V", sweep_evac)]}
        wet_free = {0: ("T", 0), 1: ("T", 0)}
        x1_free = {0: ("O", 0), 1: ("O", 0)}
        sel_free = ("O", 0)
        X1Bp = {0: X1Ba, 1: X1Bb}
        yn_free = ("G", 0)
        stmp_free = {0: ("G", 0), 1: ("G", 0)}
        OPPf = KV0f
        ZEPf = KV1f

        stA = {}
        stC = {}
        stD = {}
        oh_free = []
        xnf2_free = ("T", 0)

        def emit_A(bt):
            nonlocal xnf_free, qf_free, kpp_free
            st = {}
            xw_subs = []
            for sub in range(4):
                ts = bt * 4 + sub
                xw = S.add("Y", lambda y, ts=ts, sub=sub: y.dma_start(
                    out=XT[:, sub, :], in_=x_d[ts * 128:(ts + 1) * 128, :]),
                    waits=xt_free[sub], inc="X")
                xw_subs.append((sub, xw))
            st["xw"] = xw_subs
            xfl2 = S.add("Y", lambda y, bt=bt: y.dma_start(
                out=XNF[:, :, :, :].rearrange("p a s n -> p (a s n)"),
                in_=xnf_d[bt]),
                waits=[xnf_free, ("P", xnf_store[bt])], inc="X")
            qf_parts = []
            for jc in range(8):
                tQ = None
                for cc in range(8):
                    tQ = S.add("T", lambda t, jc=jc, cc=cc: t.matmul(
                        KPP[:, :], WA[:, cc, jc, :],
                        XNF[:, cc, :, :].rearrange("p s n -> p (s n)"),
                        start=(cc == 0), stop=(cc == 7)),
                        waits=([("X", xfl2)] + kpp_free + [("W", wWQ), qf_free, ("P", pBQ)])
                              if cc == 0 else [], inc="T" if cc == 7 else None)
                vQ = S.add("V", lambda v, jc=jc: v.tensor_scalar(
                    QF[:, jc, :], KPP[:, :], BQc[:, jc:jc + 1], None, ALU.add),
                    waits=[("T", tQ)], inc="V")
                kpp_free = [("V", vQ)]
                qf_parts.append(("V", vQ))
            st["qf"] = qf_parts
            xnf_free = ("T", S.cnt["T"])
            stA[bt] = st

        def emit_B_sub(bt, sub):
            nonlocal kpp_free, qpb_free, qfm_free, otm_free, ofm_free
            nonlocal yn_free, qf_free, trb_free
            X1B = X1Bp[bt % 2]
            st = stA[bt]
            slot, xw = st["xw"][sub]
            qf_parts = st["qf"]
            qpb_ready = []
            for hp in range(8):
                zslot = hp % 2
                zbank = ZKP if zslot == 0 else KPP
                extra = kpp_free if zslot == 1 else []
                tZ = S.add("T", lambda t, hp=hp, sub=sub, zbank=zbank: t.matmul(
                    zbank[:, :], QF[:, hp, sub * 128:(sub + 1) * 128], PD[:, hp, :],
                    start=True, stop=True),
                    waits=[qf_parts[hp], qpb_free] + zq_free[zslot] + extra, inc="T")
                vM = S.add("V", lambda v, zslot=zslot, zbank=zbank: v.reduce_max(
                    MXZ[:, zslot * 2:zslot * 2 + 2],
                    zbank[:, :].rearrange("p (h m) -> p h m", h=2),
                    axis=AX.X, negate=True),
                    waits=[("T", tZ)], inc="V")
                sE = None
                for lo in range(2):
                    sE = S.add("S", lambda s, lo=lo, zslot=zslot, zbank=zbank:
                               s.activation(KPF[:, zslot, lo * 256:(lo + 1) * 256],
                                            zbank[:, lo * 256:(lo + 1) * 256], AF.Exp,
                                            bias=MXZ[:, zslot * 2 + lo:zslot * 2 + lo + 1],
                                            scale=1.0),
                               waits=[("V", vM), kpf_free[zslot]] if lo == 0
                                     else [("V", vM)], inc="S")
                zq_free[zslot] = [("V", vM), ("S", sE)]
                if zslot == 1:
                    kpp_free = [("V", vM), ("S", sE)]
                gE = S.add("G", lambda g, hp=hp, zslot=zslot: g.tensor_scalar(
                    QPB[:, hp * 2:(hp + 1) * 2, :].rearrange("p h m -> p (h m)"),
                    KPF[:, zslot, :], INV16, FM_EPS16, ALU.mult, ALU.add),
                    waits=[("S", sE), qpb_free], inc="G")
                kpf_free[zslot] = ("G", gE)
                qpb_ready.append(("G", gE))

            qfm_ready = []
            for rnd in range(4):
                tQt = None
                for k in range(8):
                    idx = rnd * 8 + k
                    h, mc = idx // 2, idx % 2
                    tQt = S.add("T", lambda t, h=h, mc=mc, k=k: t.transpose(
                        TRB[:, k * 128:(k + 1) * 128],
                        QPB[:, h, mc * 128:(mc + 1) * 128], IDB[:, :]),
                        waits=[qpb_ready[h // 2], trb_free, qfm_free] if k == 0
                              else [qpb_ready[h // 2]],
                        inc="T" if k == 7 else None)
                vQf = S.add("V", lambda v, rnd=rnd: v.tensor_copy(
                    QFM[:, 32 + rnd * 8:32 + (rnd + 1) * 8, :],
                    TRB[:, :].rearrange("p (a n) -> p a n", a=8)),
                    waits=[("T", tQt)], inc="V")
                trb_free = ("V", vQf)
                qfm_ready.append(("V", vQf))
            qpb_free = ("T", S.cnt["T"])

            otm_ready = []
            for rnd in range(4):
                nslot = rnd % 2
                nbank = VPP if nslot == 0 else KPP
                nview = nbank[:, 0:260].rearrange("p (h m) -> p h m", h=4)
                extra = kpp_free if nslot == 1 else []
                tN = None
                for k in range(4):
                    h = rnd * 4 + k
                    for mc in range(2):
                        tN = S.add("T", lambda t, h=h, mc=mc, k=k, nview=nview: t.matmul(
                            nview[:, k, :], QFM[:, 32 + h * 2 + mc, :],
                            KVB[:, mc, h, 0:65],
                            start=(k == 0 and mc == 0), stop=(k == 3 and mc == 1)),
                            waits=([qfm_ready[rnd], kvb_ready, otm_free] +
                                   num_free[nslot] + extra)
                                  if (k == 0 and mc == 0) else [],
                            inc="T" if (k == 3 and mc == 1) else None)
                vD = S.add("V", lambda v, rnd=rnd, nview=nview: v.tensor_scalar(
                    DEN[:, rnd * 4:(rnd + 1) * 4], nview[:, :, 64:65],
                    DEN_EPS, None, ALU.add),
                    waits=[("T", tN)], inc="V")
                S.add("V", lambda v, rnd=rnd: v.reciprocal(
                    RDEN[:, rnd * 4:(rnd + 1) * 4], DEN[:, rnd * 4:(rnd + 1) * 4]),
                    waits=rt("V"), inc="V")
                vO = None
                for k in range(4):
                    h = rnd * 4 + k
                    vO = S.add("V", lambda v, h=h, k=k, nview=nview: v.tensor_scalar(
                        OTMB[:, h, :], nview[:, k, 0:64], RDEN[:, h:h + 1],
                        None, ALU.mult),
                        waits=rt("V") if k == 0 else [], inc="V")
                num_free[nslot] = [("V", vO)]
                if nslot == 1:
                    kpp_free = [("V", vO)]
                otm_ready.append(("V", vO))
            qfm_free = ("T", S.cnt["T"])

            tOt = None
            for ccg in range(8):
                tOt = S.add("T", lambda t, ccg=ccg: t.transpose(
                    TRB[:, ccg * 128:(ccg + 1) * 128],
                    OTMB[:, 2 * ccg:2 * ccg + 2, :].rearrange("p h e -> p (h e)"),
                    IDB[:, :]),
                    waits=(otm_ready + [trb_free, ofm_free]) if ccg == 0 else [],
                    inc="T" if ccg == 7 else None)
            vOf = S.add("V", lambda v: v.tensor_copy(
                OFMb[:, :, :], TRB[:, :].rearrange("p (a b) -> p a b", a=8)),
                waits=[("T", tOt)], inc="V")
            trb_free = ("V", vOf)
            otm_free = ("T", S.cnt["T"])

            ob = {0: ZKP, 1: VPP}
            tO = None
            for cc in range(8):
                for hh in range(2):
                    tO = S.add("T", lambda t, cc=cc, hh=hh, ob=ob: t.matmul(
                        ob[hh][:, :], OFMb[:, cc, :],
                        WB[:, cc, hh * 512:(hh + 1) * 512],
                        start=(cc == 0), stop=False),
                        waits=([("V", vOf), ("W", wBO)] + zq_free[0] + num_free[0])
                              if (cc == 0 and hh == 0) else [], inc=None)
            for hh in range(2):
                tO = S.add("T", lambda t, hh=hh, ob=ob: t.matmul(
                    ob[hh][:, :], ONE1[:, :],
                    BOr[:, hh * 512:(hh + 1) * 512], start=False, stop=True),
                    inc="T" if hh == 1 else None)
            ofm_free = ("T", S.cnt["T"])

            ln_bnstats(sub, ZKP[:, :], VPP[:, :], waits=[("T", tO)])
            ln_aggr(sub)
            vn = ln_combine(sub, 1)
            sY0 = S.add("S", lambda s, sub=sub: s.activation(
                YN[:, 0:512], ZKP[:, :], AF.Identity,
                bias=LNm[:, sub:sub + 1], scale=LSc[:, sub:sub + 1]),
                waits=[("V", vn), yn_free], inc="S")
            sY1 = S.add("S", lambda s, sub=sub: s.activation(
                YN[:, 512:1024], VPP[:, :], AF.Identity,
                bias=LNm[:, sub:sub + 1], scale=LSc[:, sub:sub + 1]), inc="S")
            zq_free[0] = [("S", sY1)]
            num_free[0] = [("S", sY1)]
            gX1 = S.add("G", lambda g, sub=sub, slot=slot, X1B=X1B: g.tensor_tensor(
                X1B[:, sub, :], XT[:, slot, :], YN[:, :], ALU.add),
                waits=[("S", sY1), ("X", xw), x1_free[bt % 2]], inc="G")
            yn_free = ("G", gX1)
            xt_free[slot] = [("G", gX1)]
            if sub == 3:
                qf_free = ("T", S.cnt["T"])

        def emit_C(bt):
            nonlocal sel_free, xnf2_free, trb_free, oh_free
            X1B = X1Bp[bt % 2]
            st = {}
            gX1_last = ("G", S.cnt["G"])
            for sub in range(4):
                ln_bnstats(sub, X1B[:, sub, 0:512], X1B[:, sub, 512:1024],
                           waits=[gX1_last] if sub == 0 else [])
            for sub in range(4):
                ln_aggr(sub)
            vn2 = ln_combine(0, 4)
            xn2_parts = []
            xnf2_parts = []
            for sub in range(4):
                slot = sub % 2
                sN2 = S.add("S", lambda s, sub=sub, X1B=X1B: s.activation(
                    SEL[:, sub, :], X1B[:, sub, :], AF.Identity,
                    bias=LNm[:, sub:sub + 1], scale=LSc[:, sub:sub + 1]),
                    waits=[("V", vn2), sel_free], inc="S")
                xn2_parts.append(("S", sN2))
                sC = S.add("S", lambda s, sub=sub, slot=slot, X1B=X1B: s.activation(
                    XNB[:, slot, :], X1B[:, sub, :], AF.Identity,
                    bias=LNm[:, sub:sub + 1], scale=LSc[:, sub:sub + 1]),
                    waits=[xnb_free[slot]], inc="S")
                tT = None
                for ccg in range(8):
                    tT = S.add("T", lambda t, slot=slot, ccg=ccg: t.transpose(
                        TRB[:, ccg * 128:(ccg + 1) * 128],
                        XNB[:, slot, ccg * 128:(ccg + 1) * 128], IDB[:, :]),
                        waits=[("S", sC), trb_free, xnf2_free] if ccg == 0 else [],
                        inc="T" if ccg == 7 else None)
                xnb_free[slot] = ("T", tT)
                vX2 = S.add("V", lambda v, sub=sub: v.tensor_copy(
                    XNF2[:, :, sub, :], TRB[:, :].rearrange("p (a b) -> p a b", a=8)),
                    waits=[("T", tT)], inc="V")
                trb_free = ("V", vX2)
                xnf2_parts.append(("V", vX2))
            st["xn2"] = xn2_parts
            st["xnf2"] = xnf2_parts

            gate_groups = []
            for sub in range(4):
                tGt = None
                for cc in range(8):
                    tGt = S.add("T", lambda t, sub=sub, cc=cc: t.matmul(
                        ZEPf[:, sub * 8:(sub + 1) * 8], XNF2[:, cc, sub, :], GT[:, cc, :],
                        start=(cc == 0), stop=(cc == 7)),
                        waits=([xnf2_parts[sub], ("V", vGT)] + zep_free[0] + zep_free[1])
                              if (sub == 0 and cc == 0) else
                              ([xnf2_parts[sub]] if cc == 0 else []),
                        inc="T" if cc == 7 else None)
                gate_groups.append(tGt)
            for sub in range(4):
                S.add("V", lambda v, sub=sub: v.max(
                    GMX[:, sub, :], ZEPf[:, sub * 8:(sub + 1) * 8]),
                    waits=[("T", gate_groups[sub])] + (oh_free if sub == 0 else []),
                    inc="V")
            vg = S.cnt["V"]
            for sub in range(4):
                S.add("V", lambda v, sub=sub: v.max_index(
                    GIX[:, sub, :], GMX[:, sub, :], ZEPf[:, sub * 8:(sub + 1) * 8]),
                    waits=[("V", vg)] if sub == 0 else [], inc="V")
            vg = S.cnt["V"]
            for sub in range(4):
                S.add("V", lambda v, sub=sub: v.tensor_copy(
                    IXF[:, sub:sub + 1], GIX[:, sub, 0:1]),
                    waits=[("V", vg)] if sub == 0 else [], inc="V")
            vg = S.cnt["V"]
            vOH = None
            for sub in range(4):
                vOH = S.add("V", lambda v, sub=sub: v.tensor_scalar(
                    OH[:, sub, :], IO8[:, :], IXF[:, sub:sub + 1], None, ALU.is_equal),
                    waits=[("V", vg), ("G", gIot)] if sub == 0 else [], inc="V")
            st["oh"] = ("V", vOH)
            zep_free[0] = [("V", vOH)]
            st["last_sel"] = {}
            stC[bt] = st

        def emit_D_round(bt, r):
            e, fh = r // 2, r % 2
            st = stC[bt]
            wslot = r % 2
            wE = S.add("Y", lambda y, e=e, fh=fh, wslot=wslot: y.dma_start(
                out=WET[:, wslot, :, :],
                in_=wet_d[e, :, fh * 512:(fh + 1) * 512].rearrange(
                    "(hc p) f -> p hc f", p=128)),
                waits=[wet_free[wslot], ("P", yWET_done)], inc="E")
            for sub in range(4):
                sidx = (fh * 4 + sub) % 4
                zt, zhalf = (ZEPf, sidx % 2) if sidx < 2 else (OPPf, sidx % 2)
                bankfree = zep_free[zhalf] if sidx < 2 else opp_free[zhalf]
                tZe = None
                for cc in range(8):
                    tZe = S.add("T", lambda t, cc=cc, sub=sub, wslot=wslot,
                                zt=zt, zhalf=zhalf: t.matmul(
                        zt[:, zhalf * 512:(zhalf + 1) * 512],
                        XNF2[:, cc, sub, :], WET[:, wslot, cc, :],
                        start=(cc == 0), stop=(cc == 7)),
                        waits=([("E", wE), st["xnf2"][sub], st["oh"]] + bankfree)
                              if cc == 0 else [], inc="T" if cc == 7 else None)
                selw = [("T", tZe)]
                if e > 0:
                    selw.append(st["last_sel"][(sub, fh)])
                else:
                    selw.append(st["xn2"][sub])
                src1 = SEL[:, sub, fh * 512:(fh + 1) * 512]
                dst = SEL[:, sub, fh * 512:(fh + 1) * 512]
                zsrc = zt[:, zhalf * 512:(zhalf + 1) * 512]
                ohap = OH[:, sub, e:e + 1]
                if sidx % 2 == 0:
                    sv = S.add("V", lambda v, dst=dst, zsrc=zsrc, ohap=ohap,
                               src1=src1: v.scalar_tensor_tensor(
                                   dst, zsrc, ohap, src1, ALU.mult, ALU.add),
                               waits=selw, inc="V")
                    tok = ("V", sv)
                    bank_tok = tok
                else:
                    pslot = sub % 2
                    s1 = S.add("S", lambda s, zsrc=zsrc, ohap=ohap, pslot=pslot:
                               s.activation(STMP[:, pslot, :], zsrc, AF.Identity,
                                            bias=0.0, scale=ohap),
                               waits=[("T", tZe), stmp_free[pslot]], inc="S")
                    g1 = S.add("G", lambda g, dst=dst, src1=src1, pslot=pslot:
                               g.tensor_tensor(dst, src1, STMP[:, pslot, :], ALU.add),
                               waits=[("S", s1)] + selw[1:], inc="G")
                    stmp_free[pslot] = ("G", g1)
                    tok = ("G", g1)
                    bank_tok = ("S", s1)
                st["last_sel"][(sub, fh)] = tok
                if sidx < 2:
                    zep_free[zhalf] = [bank_tok]
                else:
                    opp_free[zhalf] = [bank_tok]
            wet_free[wslot] = ("T", S.cnt["T"])

        def emit_D_tail(bt):
            nonlocal sel_free, xnf2_free, yn2_free, oh_free
            X1B = X1Bp[bt % 2]
            st = stC[bt]
            last_sel = st["last_sel"]
            for sub in range(4):
                ln_bnstats(sub, SEL[:, sub, 0:512], SEL[:, sub, 512:1024],
                           waits=[last_sel[(sub, 0)], last_sel[(sub, 1)]])
            for sub in range(4):
                ln_aggr(sub)
            vn3 = ln_combine(0, 4)
            gOut = None
            for sub in range(4):
                sY2 = S.add("S", lambda s, sub=sub: s.activation(
                    YN2[:, :], SEL[:, sub, :], AF.Identity,
                    bias=LNm[:, sub:sub + 1], scale=LSc[:, sub:sub + 1]),
                    waits=[("V", vn3), yn2_free], inc="S")
                gOut = S.add("G", lambda g, sub=sub, X1B=X1B: g.tensor_tensor(
                    X1B[:, sub, :], X1B[:, sub, :], YN2[:, :], ALU.add),
                    waits=[("S", sY2)], inc="G")
                yn2_free = ("G", gOut)
            yO = S.add("Y", lambda y, bt=bt, X1B=X1B: y.dma_start(
                out=out_d[bt * 512:(bt + 1) * 512, :].rearrange("(s p) c -> p s c", p=128),
                in_=X1B[:, :, :]),
                waits=[("G", gOut)], inc="O")
            x1_free[bt % 2] = ("O", yO)
            sel_free = ("O", yO)
            xnf2_free = ("T", S.cnt["T"])
            oh_free = list(last_sel.values())

        yn2_free = ("G", 0)
        for bt in range(NB2):
            emit_A(bt)
            for sub in range(4):
                emit_B_sub(bt, sub)
                if bt > 0:
                    for r in range(sub * 4, sub * 4 + 4):
                        emit_D_round(bt - 1, r)
            if bt > 0:
                emit_D_tail(bt - 1)
            emit_C(bt)
        for r in range(16):
            emit_D_round(NB2 - 1, r)
        emit_D_tail(NB2 - 1)

        if dbg:
            allw = [("T", S.cnt["T"]), ("V", S.cnt["V"]), ("S", S.cnt["S"]),
                    ("G", S.cnt["G"])]
            dbg_list = [
                ("d_STs", STs, F32), ("d_STn", STn, F32),
                ("d_KVB", KVB, BF16), ("d_QF", QF, BF16),
                ("d_KPB", KPB, BF16), ("d_OTMB", OTMB, BF16),
                ("d_OH", OH, F32),
                ("d_SEL", KVS, F32), ("d_DEN", DEN, F32),
                ("d_XNF", XNF, BF16), ("d_X1B", X1Ba, F32),
                ("d_PD", PD, BF16), ("d_GT", GT, BF16),
                ("d_VE", VE, BF16), ("d_KF", KF, BF16),
                ("d_BQc", BQc, F32), ("d_BKc", BKc, F32),
                ("d_RDEN", RDEN, F32), ("d_MV", MV, F32),
                ("d_LSc", LSc, F32), ("d_QFb", QF, BF16),
            ]
            for nm, buf, dt in dbg_list:
                shp = [128] + [int(s) for s in buf.shape[1:]]
                dd = nc.dram_tensor(nm, shp, dt, kind="ExternalOutput").ap()
                S.add("Y", lambda y, dd=dd, buf=buf: y.dma_start(out=dd[:], in_=buf[:]),
                      waits=allw, inc="O")
        S.add("Y", lambda y: y.nop(), waits=[("O", S.cnt["O"])])

        with nc.Block() as block:
            @block.tensor
            def _(t):
                S.emit(t, "T", sems)

            @block.vector
            def _(v):
                S.emit(v, "V", sems)

            @block.scalar
            def _(s):
                S.emit(s, "S", sems)

            @block.gpsimd
            def _(g):
                S.emit(g, "G", sems)

            @block.sync
            def _(y):
                S.emit(y, "Y", sems)

    return nc


def kernel(L0_attn_x, L0_attn_w_qkv, L0_attn_b_qkv, L0_attn_w_out, L0_attn_b_out,
           L0_attn_proj_matrix, L0_ffn_gate_w, L0_ffn_expert_w):
    x = np.ascontiguousarray(np.asarray(L0_attn_x, dtype=np.float32))
    wqkv = np.ascontiguousarray(np.asarray(L0_attn_w_qkv, dtype=np.float32))
    bqkv = np.ascontiguousarray(np.asarray(L0_attn_b_qkv, dtype=np.float32))
    wout = np.ascontiguousarray(np.asarray(L0_attn_w_out, dtype=np.float32))
    bout = np.ascontiguousarray(np.asarray(L0_attn_b_out, dtype=np.float32))
    proj = np.ascontiguousarray(np.asarray(L0_attn_proj_matrix, dtype=np.float32))
    gate = np.ascontiguousarray(np.asarray(L0_ffn_gate_w, dtype=np.float32))
    expw = np.ascontiguousarray(np.asarray(L0_ffn_expert_w, dtype=np.float32))

    nc = build_program()
    in_maps = []
    for c in range(8):
        b, hf = c // 2, c % 2
        own = x[b, hf * 4096:(hf + 1) * 4096]
        other = x[b, (1 - hf) * 4096:(2 - hf) * 4096]
        xp = np.ascontiguousarray(np.concatenate([own, other], axis=0))
        in_maps.append({
            "x_seq": xp, "w_qkv": wqkv, "b_qkv": bqkv, "w_out": wout,
            "b_out": bout, "proj": proj, "gate_w": gate, "expert_w": expw,
        })
    res = run_bass_kernel_spmd(nc, in_maps, core_ids=list(range(8)))
    out = np.empty((4, 8192, C), dtype=np.float32)
    for c in range(8):
        b, hf = c // 2, c % 2
        out[b, hf * 4096:(hf + 1) * 4096] = res.results[c]["out"]
    return out


# version 10
# speedup vs baseline: 1.8786x; 1.0437x over previous
import numpy as np
from contextlib import ExitStack

import concourse.bass as bass
import concourse.mybir as mybir
from concourse.bass_utils import run_bass_kernel_spmd

F32 = mybir.dt.float32
BF16 = mybir.dt.bfloat16
U32 = mybir.dt.uint32
AF = mybir.ActivationFunctionType
ALU = mybir.AluOpType
AX = mybir.AxisListType

C = 1024
H = 16
E = 8
LN_EPS = 1e-5
FM_EPS16 = 1e-6 / 16.0
DEN_EPS = 1e-6
DN = float(64.0 ** (-0.25))
INV16 = 1.0 / 16.0

DMA_SEMS = frozenset("XWEOP")
SEM_NAMES = ["T", "V", "S", "G", "X", "W", "E", "O", "P"]


class Sched:
    def __init__(self):
        self.cnt = {s: 0 for s in SEM_NAMES}
        self.ops = {e: [] for e in "TVSGY"}

    def add(self, eng, fn, waits=(), inc=None, amt=1):
        ws = []
        for w in waits:
            if w is None:
                continue
            s, v = w
            if v and v > 0:
                ws.append((s, v))
        a = 0
        val = None
        if inc is not None:
            a = 16 if inc in DMA_SEMS else amt
            self.cnt[inc] += a
            val = self.cnt[inc]
        self.ops[eng].append((fn, ws, inc, a))
        return val

    def emit(self, eng_obj, eng_name, sems):
        maxw = {}
        for fn, ws, inc, a in self.ops[eng_name]:
            for s, v in ws:
                if s == eng_name and s == "T":
                    continue
                if maxw.get(s, 0) >= v:
                    continue
                maxw[s] = v
                eng_obj.wait_ge(sems[s], v)
            instr = fn(eng_obj)
            if inc is not None:
                instr.then_inc(sems[inc], a)


def build_program(NB1=16, NB2=8, NE=8, dbg=False):
    nc = bass.Bass("TRN2", target_bir_lowering=False)

    x_d = nc.dram_tensor("x_seq", [8192, C], F32, kind="ExternalInput").ap()
    wqkv_d = nc.dram_tensor("w_qkv", [C, 3 * C], F32, kind="ExternalInput").ap()
    bqkv_d = nc.dram_tensor("b_qkv", [3 * C], F32, kind="ExternalInput").ap()
    wout_d = nc.dram_tensor("w_out", [C, C], F32, kind="ExternalInput").ap()
    bout_d = nc.dram_tensor("b_out", [C], F32, kind="ExternalInput").ap()
    proj_d = nc.dram_tensor("proj", [H, 256, 64], F32, kind="ExternalInput").ap()
    gate_d = nc.dram_tensor("gate_w", [E, C], F32, kind="ExternalInput").ap()
    expw_d = nc.dram_tensor("expert_w", [E, C, C], F32, kind="ExternalInput").ap()
    out_d = nc.dram_tensor("out", [4096, C], F32, kind="ExternalOutput").ap()
    wet_d = nc.dram_tensor("wet_scratch", [E, C, C], BF16, kind="Internal").ap()
    xnf_d = nc.dram_tensor("xnf_scr", [16, 128, 4096], BF16, kind="Internal").ap()

    es = ExitStack()
    S = Sched()

    def sb(name, shape, dt=F32):
        return es.enter_context(nc.sbuf_tensor(name, shape, dt))

    def psum(name, shape, dt=F32):
        return es.enter_context(nc.psum_tensor(name, shape, dt))

    with es:
        WA = sb("WA", [128, 8, 8, 128], BF16)
        WB = sb("WB", [128, 8, C], BF16)
        PD = sb("PD", [128, 8, 512], BF16)
        GT = sb("GT", [128, 8, 8], BF16)
        BQc = sb("BQc", [128, 8], F32)
        BKc = sb("BKc", [128, 8], F32)
        BVr = sb("BVr", [1, C], BF16)
        BOr = sb("BOr", [1, C], BF16)
        ONE1 = sb("ONE1", [1, 128], BF16)
        IDB = sb("IDB", [128, 128], BF16)
        IDF = sb("IDF", [128, 128], F32)
        IO8 = sb("IO8", [128, 8], F32)
        STs = sb("STs", [128, 64], F32)
        STn = sb("STn", [128, 64], F32)
        KVB = sb("KVB", [128, 2, H, 68], BF16)
        GW8 = sb("GW8", [8, C], BF16)

        XT = sb("XT", [128, 4, C], F32)
        XNB = sb("XNB", [128, 2, C], BF16)
        XNF = sb("XNF", [128, 8, 4, 128], BF16)
        XNF2 = sb("XNF2", [128, 8, 4, 128], BF16)
        YN2 = sb("YN2", [128, C], F32)
        KF = sb("KF", [128, 4, 512], BF16)
        VE = sb("VE", [128, 4, 8, 68], BF16)
        KPF = sb("KPF", [128, 2, 512], F32)
        KPB = sb("KPB", [128, 8192], BF16)
        KVS = sb("KVS", [128, 4096], F32)
        QF = sb("QF", [128, 8, 512], BF16)
        X1Ba = sb("X1Ba", [128, 4, C], F32)
        X1Bb = sb("X1Bb", [128, 4, C], F32)
        YN = sb("YN", [128, C], F32)
        OH = sb("OH", [128, 4, 8], F32)
        WET = sb("WET", [128, 2, 8, 512], BF16)
        OTMB = sb("OTMB", [128, 16, 64], BF16)
        DEN = sb("DEN", [128, 16], F32)
        RDEN = sb("RDEN", [128, 16], F32)
        MXZ = sb("MXZ", [128, 4], F32)
        GMX = sb("GMX", [128, 4, 8], F32)
        GIX = sb("GIX", [128, 4, 8], U32)
        IXF = sb("IXF", [128, 4], F32)

        STMP = sb("STMP", [128, 2, 512], F32)
        BNS = sb("BNS", [128, 4, 12], F32)
        MV = sb("MV", [128, 4, 2, 2], F32)
        LT0 = sb("LT0", [128, 4], F32)
        LT1 = sb("LT1", [128, 4], F32)
        LT2 = sb("LT2", [128, 4], F32)
        LT3 = sb("LT3", [128, 4], F32)
        LSc = sb("LSc", [128, 4], F32)
        LNm = sb("LNm", [128, 4], F32)

        TRB = psum("TRB", [128, 1024], BF16)
        KPP = psum("KPP", [128, 512], F32)
        VPP = psum("VPP", [128, 512], F32)
        ZKP = psum("ZKP", [128, 512], F32)
        KV0f = psum("KV0f", [128, 1024], F32)
        KV1f = psum("KV1f", [128, 1024], F32)
        kv_of = {0: KV0f, 1: KV1f}

        sems = {}
        for name in SEM_NAMES:
            sems[name] = es.enter_context(nc.semaphore("sem" + name))

        def rt(eng):
            return [(eng, S.cnt[eng])]

        def ln_bnstats(col, src0, src1, waits):
            S.add("V", lambda v, a=src0, c=col: v.bn_stats(BNS[:, c, 0:6], a),
                  waits=waits, inc="V")
            return S.add("V", lambda v, a=src1, c=col: v.bn_stats(BNS[:, c, 6:12], a),
                         inc="V")

        def ln_aggr(col):
            S.add("V", lambda v, c=col: v.bn_aggr(MV[:, c, 0, :], BNS[:, c, 0:6]), inc="V")
            return S.add("V", lambda v, c=col: v.bn_aggr(MV[:, c, 1, :], BNS[:, c, 6:12]),
                         waits=rt("V"), inc="V")

        def ln_combine(c0, n):
            cs = slice(c0, c0 + n)
            m0 = MV[:, cs, 0:1, 0:1]
            m1 = MV[:, cs, 1:2, 0:1]
            v0 = MV[:, cs, 0:1, 1:2]
            v1 = MV[:, cs, 1:2, 1:2]
            S.add("V", lambda v: v.tensor_tensor(LT0[:, cs], m0, m1, ALU.subtract),
                  waits=rt("V"), inc="V")
            S.add("V", lambda v: v.tensor_tensor(LT1[:, cs], v0, v1, ALU.add), inc="V")
            S.add("V", lambda v: v.tensor_tensor(LT3[:, cs], m0, m1, ALU.add), inc="V")
            S.add("V", lambda v: v.tensor_tensor(LT2[:, cs], LT0[:, cs], LT0[:, cs], ALU.mult),
                  waits=rt("V"), inc="V")
            S.add("V", lambda v: v.tensor_scalar(LT1[:, cs], LT1[:, cs], 0.5, LN_EPS,
                                                 ALU.mult, ALU.add), inc="V")
            vv = S.add("V", lambda v: v.scalar_tensor_tensor(LT0[:, cs], LT2[:, cs], 0.25,
                                                             LT1[:, cs], ALU.mult, ALU.add),
                       waits=rt("V"), inc="V")
            sv = S.add("S", lambda s: s.activation(LT2[:, cs], LT0[:, cs], AF.Ln,
                                                   bias=0.0, scale=1.0),
                       waits=[("V", vv)], inc="S")
            sv2 = S.add("S", lambda s: s.activation(LSc[:, cs], LT2[:, cs], AF.Exp,
                                                    bias=0.0, scale=-0.5),
                        waits=[("S", sv)], inc="S")
            S.add("V", lambda v: v.tensor_tensor(LT3[:, cs], LT3[:, cs], LSc[:, cs], ALU.mult),
                  waits=[("S", sv2)], inc="V")
            return S.add("V", lambda v: v.tensor_scalar(LNm[:, cs], LT3[:, cs], -0.5, None,
                                                        ALU.mult),
                         waits=rt("V"), inc="V")

        pBQ = S.add("Y", lambda y: y.dma_start(
            out=BQc[:, :], in_=bqkv_d[0:1024].rearrange("(jc p) -> p jc", p=128),
            allow_slow_non_contiguous=True), inc="P")
        pBK = S.add("Y", lambda y: y.dma_start(
            out=BKc[:, :], in_=bqkv_d[1024:2048].rearrange("(jc p) -> p jc", p=128),
            allow_slow_non_contiguous=True), inc="P")

        wWK = None
        for cc in range(8):
            wWK = S.add("G", lambda g, cc=cc: g.dma_start(
                out=WA[:, cc, :, :].rearrange("p a b -> p (a b)"),
                in_=wqkv_d[cc * 128:(cc + 1) * 128, 1024:2048]), inc="W")
        wWV = None
        for cc in range(8):
            wWV = S.add("G", lambda g, cc=cc: g.dma_start(
                out=WB[:, cc, :], in_=wqkv_d[cc * 128:(cc + 1) * 128, 2048:3072]), inc="W")
        wBV = S.add("G", lambda g: g.dma_start(out=BVr[:, :], in_=bqkv_d[None, 2048:3072]),
                    inc="W")
        wGW = S.add("G", lambda g: g.dma_start(out=GW8[:, :], in_=gate_d[:, :]), inc="W")

        S.add("G", lambda g: g.memset(ONE1[:, :], 1.0), inc="G")
        S.add("G", lambda g: g.iota(XT[:, 0, 0:128], [[1, 128]], base=0, channel_multiplier=0,
                                    allow_small_or_imprecise_dtypes=True), inc="G")
        S.add("G", lambda g: g.iota(XT[:, 0, 128:256], [[0, 128]], base=0, channel_multiplier=1,
                                    allow_small_or_imprecise_dtypes=True), inc="G")
        gIot = S.add("G", lambda g: g.iota(IO8[:, :], [[1, 8]], base=0, channel_multiplier=0,
                                           allow_small_or_imprecise_dtypes=True), inc="G")
        gPD0 = S.add("G", lambda g: g.memset(PD[:, :, :], 0.0), inc="G")
        gVE1 = S.add("G", lambda g: g.memset(VE[:, :, :, 64:68], 1.0), inc="G")

        S.add("V", lambda v: v.tensor_tensor(IDF[:, :], XT[:, 0, 0:128], XT[:, 0, 128:256],
                                             ALU.is_equal),
              waits=[("G", gIot)], inc="V")
        vIDb = S.add("V", lambda v: v.tensor_copy(IDB[:, :], IDF[:, :]),
                     waits=rt("V"), inc="V")

        trb_free = ("V", 0)
        vPD = 0
        for h in range(H):
            wPJ = S.add("G", lambda g, h=h: g.dma_start(
                out=XNB[:, 1, 0:128].rearrange("p (a b) -> p a b", a=2),
                in_=proj_d[h].rearrange("(mc p) d -> p mc d", p=128)),
                waits=[("V", vPD)], inc="W")
            gSc = S.add("G", lambda g: g.tensor_scalar(XNB[:, 1, 0:128], XNB[:, 1, 0:128],
                                                       DN, None, ALU.mult),
                        waits=[("W", wPJ)], inc="G")
            tTR = None
            for mc in range(2):
                tTR = S.add("T", lambda t, mc=mc: t.transpose(
                    TRB[0:64, mc * 128:(mc + 1) * 128],
                    XNB[:, 1, mc * 64:(mc + 1) * 64], IDB[:, :]),
                    waits=[("G", gSc), ("V", vIDb), trb_free] if mc == 0 else [],
                    inc="T" if mc == 1 else None)
            hp, lo = h // 2, h % 2
            vPD = S.add("V", lambda v, hp=hp, lo=lo: v.tensor_copy(
                PD[lo * 64:(lo + 1) * 64, hp, lo * 256:(lo + 1) * 256], TRB[0:64, 0:256]),
                waits=[("T", tTR), ("G", gPD0)], inc="V")
            trb_free = ("V", vPD)

        tG = None
        for cc in range(8):
            tG = S.add("T", lambda t, cc=cc: t.transpose(
                TRB[:, 256 + cc * 8:256 + (cc + 1) * 8],
                GW8[0:8, cc * 128:(cc + 1) * 128], IDB[0:8, 0:8]),
                waits=[("W", wGW), trb_free] if cc == 0 else [],
                inc="T" if cc == 7 else None)
        vGT = S.add("V", lambda v: v.tensor_copy(
            GT[:, :, :].rearrange("p a b -> p (a b)"), TRB[:, 256:320]),
            waits=[("T", tG)], inc="V")
        trb_free = ("V", vGT)

        WETB = KPB.rearrange("p (hc f) -> p hc f", hc=8)
        xnb_free = {0: ("T", 0), 1: ("T", S.cnt["T"])}
        yWET = 0
        for e in range(NE):
            vEv = None
            for fc in range(8):
                slot = fc % 2
                wEW = S.add("G", lambda g, e=e, slot=slot, fc=fc: g.dma_start(
                    out=XNB[:, slot, :], in_=expw_d[e, fc * 128:(fc + 1) * 128, :]),
                    waits=[xnb_free[slot]], inc="W")
                tE = None
                for hc in range(8):
                    tE = S.add("T", lambda t, slot=slot, hc=hc, wEW=wEW: t.transpose(
                        TRB[:, hc * 128:(hc + 1) * 128],
                        XNB[:, slot, hc * 128:(hc + 1) * 128], IDB[:, :]),
                        waits=[("W", wEW), trb_free] if hc == 0 else [],
                        inc="T" if hc == 7 else None)
                xnb_free[slot] = ("T", tE)
                vEv = S.add("V", lambda v, fc=fc: v.tensor_copy(
                    WETB[:, :, fc * 128:(fc + 1) * 128],
                    TRB[:, :].rearrange("p (a b) -> p a b", a=8)),
                    waits=[("T", tE)] + ([("P", yWET)] if fc == 0 else []), inc="V")
                trb_free = ("V", vEv)
            yWET = S.add("Y", lambda y, e=e: y.dma_start(
                out=wet_d[e].rearrange("(hc p) f -> p hc f", p=128), in_=WETB[:, :, :]),
                waits=[("V", vEv)], inc="P")
        yWET_done = S.cnt["P"]
        kpb_free = ("P", yWET_done)

        xt_free = {0: [("V", vIDb)], 1: [("S", 0)], 2: [("S", 0)], 3: [("S", 0)]}
        xnf_free = ("T", 0)
        kf_free = ("T", 0)
        ve_free = ("T", 0)
        kpf_free = {0: ("G", 0), 1: ("G", 0)}
        kpp_free = [("V", 0)]
        vpp_free = [("V", 0)]
        zk_free = {0: [("V", 0), ("S", 0)], 1: [("V", 0), ("S", 0)]}
        sweep_evac = 0

        xnf_store = {}
        for sweep in range(2):
            kv_started = set()
            last_kv = 0
            for bt in range(NB1):
                if sweep == 0:
                    sub_info = []
                    for sub in range(4):
                        ts = bt * 4 + sub
                        slot = sub
                        xw = S.add("Y", lambda y, ts=ts, slot=slot: y.dma_start(
                            out=XT[:, slot, :], in_=x_d[ts * 128:(ts + 1) * 128, :]),
                            waits=xt_free[slot] +
                                  ([("W", wBV), ("P", pBK)] if bt == 0 else []),
                            inc="X")
                        vbn = ln_bnstats(sub, XT[:, slot, 0:512], XT[:, slot, 512:1024],
                                         waits=[("X", xw)])
                        sub_info.append((slot, xw, ts, vbn))
                    for sub in range(4):
                        ln_aggr(sub)
                    ln_combine(0, 4)
                    S.add("V", lambda v, bt=bt: v.tensor_copy(
                        STs[:, bt * 4:(bt + 1) * 4], LSc[:, 0:4]), waits=rt("V"), inc="V")
                    vstat = S.add("V", lambda v, bt=bt: v.tensor_copy(
                        STn[:, bt * 4:(bt + 1) * 4], LNm[:, 0:4]), inc="V")

                    xnf_parts = []
                    for sub in range(4):
                        slot, xw, ts, vbn = sub_info[sub]
                        ns = sub % 2
                        sN = S.add("S", lambda s, slot=slot, ns=ns, ts=ts: s.activation(
                            XNB[:, ns, :], XT[:, slot, :], AF.Identity,
                            bias=STn[:, ts:ts + 1], scale=STs[:, ts:ts + 1]),
                            waits=[("X", xw), ("V", vstat), xnb_free[ns]], inc="S")
                        xt_free[slot] = [("S", sN), ("V", vbn)]
                        tT = None
                        for ccg in range(8):
                            tT = S.add("T", lambda t, ns=ns, ccg=ccg: t.transpose(
                                TRB[:, ccg * 128:(ccg + 1) * 128],
                                XNB[:, ns, ccg * 128:(ccg + 1) * 128], IDB[:, :]),
                                waits=[("S", sN), trb_free, xnf_free] if ccg == 0 else [],
                                inc="T" if ccg == 7 else None)
                        xnb_free[ns] = ("T", tT)
                        vX = S.add("V", lambda v, sub=sub: v.tensor_copy(
                            XNF[:, :, sub, :], TRB[:, :].rearrange("p (a b) -> p a b", a=8)),
                            waits=[("T", tT)], inc="V")
                        trb_free = ("V", vX)
                        xnf_parts.append(("V", vX))
                    xnf_store[bt] = S.add("Y", lambda y, bt=bt: y.dma_start(
                        out=xnf_d[bt],
                        in_=XNF[:, :, :, :].rearrange("p a s n -> p (a s n)")),
                        waits=list(xnf_parts), inc="P")
                else:
                    xfl = S.add("Y", lambda y, bt=bt: y.dma_start(
                        out=XNF[:, :, :, :].rearrange("p a s n -> p (a s n)"),
                        in_=xnf_d[bt]),
                        waits=[xnf_free, ("P", xnf_store[bt])], inc="X")
                    xnf_parts = [("X", xfl)]

                kf_ready = []
                ve_ready = []
                for r in range(4):
                    tK = None
                    for cc in range(8):
                        tK = S.add("T", lambda t, r=r, cc=cc, sweep=sweep: t.matmul(
                            KPP[:, :], WA[:, cc, sweep * 4 + r, :],
                            XNF[:, cc, :, :].rearrange("p s n -> p (s n)"),
                            start=(cc == 0), stop=(cc == 7)),
                            waits=(xnf_parts + kpp_free + [kf_free, ("W", wWK)])
                                  if cc == 0 else [], inc="T" if cc == 7 else None)
                    vK = S.add("V", lambda v, r=r, sweep=sweep: v.tensor_scalar(
                        KF[:, r, :], KPP[:, :],
                        BKc[:, sweep * 4 + r:sweep * 4 + r + 1], None, ALU.add),
                        waits=[("T", tK)], inc="V")
                    kpp_free = [("V", vK)]
                    kf_ready.append(("V", vK))
                    wa_last = ("T", tK)

                    tV = None
                    for cc in range(8):
                        tV = S.add("T", lambda t, r=r, cc=cc, sweep=sweep: t.matmul(
                            VPP[:, :], XNF[:, cc, r, :],
                            WB[:, cc, sweep * 512:(sweep + 1) * 512],
                            start=(cc == 0), stop=False),
                            waits=(vpp_free + [ve_free, ("W", wWV)]) if cc == 0 else [],
                            inc=None)
                    tV = S.add("T", lambda t, sweep=sweep: t.matmul(
                        VPP[:, :], ONE1[:, :], BVr[:, sweep * 512:(sweep + 1) * 512],
                        start=False, stop=True), inc="T")
                    wb_last = ("T", S.cnt["T"])
                    vV = S.add("V", lambda v, r=r: v.tensor_copy(
                        VE[:, r, :, 0:64], VPP[:, :].rearrange("p (h d) -> p h d", h=8)),
                        waits=[("T", tV), ("G", gVE1)], inc="V")
                    vpp_free = [("V", vV)]
                    ve_ready.append(("V", vV))

                KPBv = KPB.rearrange("p (s h m) -> p s h m", s=4, h=8)
                kpb_ready = []

                def emit_kv(p):
                    nonlocal last_kv
                    sub, hp = p // 4, p % 4
                    for h in (hp * 2, hp * 2 + 1):
                        kvt = kv_of[h // 4]
                        kvv = kvt.rearrange("p (q m) -> p q m", q=4)
                        hq = h % 4
                        bank_key = (h // 4, hq // 2)
                        st = bank_key not in kv_started
                        kv_started.add(bank_key)
                        last = (bt == NB1 - 1 and sub == 3 and (hq % 2 == 1))
                        last_kv = S.add("T", lambda t, sub=sub, h=h, kvv=kvv, hq=hq,
                                        st=st, last=last: t.matmul(
                            kvv[0:65, hq, :], VE[:, sub, h, 0:65], KPBv[:, sub, h, :],
                            start=st, stop=last),
                            waits=[kpb_ready[p], ve_ready[sub],
                                   ("V", sweep_evac) if st else None],
                            inc="T" if (sub == 3 and h == 7) else None) or last_kv

                for p in range(18):
                    if p < 16:
                        sub, hp = p // 4, p % 4
                        zslot = p % 2
                        zbank = ZKP if zslot == 0 else KPP
                        extra = kpp_free if zslot == 1 else []
                        tZ = S.add("T", lambda t, sub=sub, hp=hp, sweep=sweep, zbank=zbank:
                                   t.matmul(zbank[:, :], KF[:, hp, sub * 128:(sub + 1) * 128],
                                            PD[:, sweep * 4 + hp, :], start=True, stop=True),
                                   waits=[kf_ready[hp]] + zk_free[zslot] + extra, inc="T")
                        vM = S.add("V", lambda v, zslot=zslot, zbank=zbank: v.reduce_max(
                            MXZ[:, zslot * 2:zslot * 2 + 2],
                            zbank[:, :].rearrange("p (h m) -> p h m", h=2),
                            axis=AX.X, negate=True),
                            waits=[("T", tZ)], inc="V")
                        sE = None
                        for lo in range(2):
                            sE = S.add("S", lambda s, lo=lo, zslot=zslot, zbank=zbank:
                                       s.activation(KPF[:, zslot, lo * 256:(lo + 1) * 256],
                                                    zbank[:, lo * 256:(lo + 1) * 256],
                                                    AF.Exp, bias=MXZ[:, zslot * 2 + lo:zslot * 2 + lo + 1],
                                                    scale=1.0),
                                       waits=[("V", vM), kpf_free[zslot]] if lo == 0
                                             else [("V", vM)], inc="S")
                        zk_free[zslot] = [("V", vM), ("S", sE)]
                        if zslot == 1:
                            kpp_free = [("V", vM), ("S", sE)]
                        gE = S.add("G", lambda g, sub=sub, hp=hp, zslot=zslot: g.tensor_scalar(
                            KPBv[:, sub, hp * 2:(hp + 1) * 2, :].rearrange("p h m -> p (h m)"),
                            KPF[:, zslot, :], INV16, FM_EPS16, ALU.mult, ALU.add),
                            waits=[("S", sE), kpb_free], inc="G")
                        kpf_free[zslot] = ("G", gE)
                        kpb_ready.append(("G", gE))
                    if p >= 2:
                        emit_kv(p - 2)
                kf_free = ("T", S.cnt["T"])
                ve_free = ("T", S.cnt["T"])
                kpb_free = ("T", S.cnt["T"])
                xnf_free = ("T", S.cnt["T"])

            KVSv = KVS.rearrange("p (h m) -> p h m", h=16)
            vEv = None
            for h in range(8):
                kvt = kv_of[h // 4]
                kvv = kvt.rearrange("p (q m) -> p q m", q=4)
                vEv = S.add("V", lambda v, h=h, kvv=kvv, sweep=sweep: v.tensor_copy(
                    KVSv[0:65, sweep * 8 + h, :], kvv[0:65, h % 4, :]),
                    waits=[("T", last_kv)] if h == 0 else [], inc="V")
            sweep_evac = vEv

        KVSv = KVS.rearrange("p (h m) -> p h m", h=16)
        vKb = ("V", 0)
        for h in range(H):
            for mc in range(2):
                tKt = S.add("T", lambda t, h=h, mc=mc: t.transpose(
                    KPP[:, 0:65], KVSv[0:65, h, mc * 128:(mc + 1) * 128], IDF[0:65, 0:65]),
                    waits=[("V", sweep_evac), vKb] + kpp_free, inc="T")
                vKb = ("V", S.add("V", lambda v, h=h, mc=mc: v.tensor_copy(
                    KVB[:, mc, h, 0:65], KPP[:, 0:65]),
                    waits=[("T", tKt)], inc="V"))
                kpp_free = []
        kvb_ready = vKb
        kpp_free = [vKb]

        ph1_T_done = ("T", S.cnt["T"])
        wWQ = None
        for cc in range(8):
            wWQ = S.add("G", lambda g, cc=cc: g.dma_start(
                out=WA[:, cc, :, :].rearrange("p a b -> p (a b)"),
                in_=wqkv_d[cc * 128:(cc + 1) * 128, 0:1024]),
                waits=[wa_last] if cc == 0 else [], inc="W")
        wWO = None
        for cc in range(8):
            wWO = S.add("G", lambda g, cc=cc: g.dma_start(
                out=WB[:, cc, :], in_=wout_d[cc * 128:(cc + 1) * 128, :]),
                waits=[wb_last] if cc == 0 else [], inc="W")
        wBO = S.add("G", lambda g: g.dma_start(out=BOr[:, :], in_=bout_d[None, :]), inc="W")

        QPB = KPB.rearrange("p (h m) -> p h m", h=32)
        QFM = KPB.rearrange("p (g n) -> p g n", g=64)
        SEL = KVS.rearrange("p (s f) -> p s f", s=4)
        OTM = VE.rearrange("p s h e -> p (s h) e")
        OFMb = KF.rearrange("p a b -> p (a b)")[:, 0:1024].rearrange("p (g n) -> p g n", g=8)

        xnf_free = ph1_T_done
        qf_free = ("T", 0)
        qpb_free = ph1_T_done
        qfm_free = ("T", 0)
        otm_free = ("T", 0)
        ofm_free = ("T", 0)
        zq_free = zk_free
        qpf_free = dict(kpf_free)
        num_free = {0: list(vpp_free), 1: [("V", 0)]}
        opp_free = {0: [("V", sweep_evac)], 1: [("V", sweep_evac)]}
        zep_free = {0: [("V", sweep_evac)], 1: [("V", sweep_evac)]}
        wet_free = {0: ("T", 0), 1: ("T", 0)}
        x1_free = {0: ("O", 0), 1: ("O", 0)}
        sel_free = ("O", 0)
        X1Bp = {0: X1Ba, 1: X1Bb}
        yn_free = ("G", 0)
        stmp_free = {0: ("G", 0), 1: ("G", 0)}
        OPPf = KV0f
        ZEPf = KV1f

        stA = {}
        stC = {}
        stD = {}
        oh_free = []
        xnf2_free = ("T", 0)

        def emit_A(bt):
            nonlocal xnf_free, qf_free, kpp_free
            st = {}
            xw_subs = []
            for sub in range(4):
                ts = bt * 4 + sub
                xw = S.add("Y", lambda y, ts=ts, sub=sub: y.dma_start(
                    out=XT[:, sub, :], in_=x_d[ts * 128:(ts + 1) * 128, :]),
                    waits=xt_free[sub], inc="X")
                xw_subs.append((sub, xw))
            st["xw"] = xw_subs
            xfl2 = S.add("Y", lambda y, bt=bt: y.dma_start(
                out=XNF[:, :, :, :].rearrange("p a s n -> p (a s n)"),
                in_=xnf_d[bt]),
                waits=[xnf_free, ("P", xnf_store[bt])], inc="X")
            qf_parts = []
            for jc in range(8):
                tQ = None
                for cc in range(8):
                    tQ = S.add("T", lambda t, jc=jc, cc=cc: t.matmul(
                        KPP[:, :], WA[:, cc, jc, :],
                        XNF[:, cc, :, :].rearrange("p s n -> p (s n)"),
                        start=(cc == 0), stop=(cc == 7)),
                        waits=([("X", xfl2)] + kpp_free + [("W", wWQ), qf_free, ("P", pBQ)])
                              if cc == 0 else [], inc="T" if cc == 7 else None)
                vQ = S.add("V", lambda v, jc=jc: v.tensor_scalar(
                    QF[:, jc, :], KPP[:, :], BQc[:, jc:jc + 1], None, ALU.add),
                    waits=[("T", tQ)], inc="V")
                kpp_free = [("V", vQ)]
                qf_parts.append(("V", vQ))
            st["qf"] = qf_parts
            xnf_free = ("T", S.cnt["T"])
            stA[bt] = st

        def emit_B_sub(bt, sub, drain):
            nonlocal kpp_free, qpb_free, qfm_free, otm_free, ofm_free
            nonlocal yn_free, qf_free, trb_free
            X1B = X1Bp[bt % 2]
            st = stA[bt]
            slot, xw = st["xw"][sub]
            qf_parts = st["qf"]
            qpb_ready = []
            for hp in range(8):
                zslot = hp % 2
                zbank = ZKP if zslot == 0 else KPP
                extra = kpp_free if zslot == 1 else []
                tZ = S.add("T", lambda t, hp=hp, sub=sub, zbank=zbank: t.matmul(
                    zbank[:, :], QF[:, hp, sub * 128:(sub + 1) * 128], PD[:, hp, :],
                    start=True, stop=True),
                    waits=[qf_parts[hp], qpb_free] + zq_free[zslot] + extra, inc="T")
                vM = S.add("V", lambda v, zslot=zslot, zbank=zbank: v.reduce_max(
                    MXZ[:, zslot * 2:zslot * 2 + 2],
                    zbank[:, :].rearrange("p (h m) -> p h m", h=2),
                    axis=AX.X, negate=True),
                    waits=[("T", tZ)], inc="V")
                sE = None
                for lo in range(2):
                    sE = S.add("S", lambda s, lo=lo, zslot=zslot, zbank=zbank:
                               s.activation(KPF[:, zslot, lo * 256:(lo + 1) * 256],
                                            zbank[:, lo * 256:(lo + 1) * 256], AF.Exp,
                                            bias=MXZ[:, zslot * 2 + lo:zslot * 2 + lo + 1],
                                            scale=1.0),
                               waits=[("V", vM), kpf_free[zslot]] if lo == 0
                                     else [("V", vM)], inc="S")
                zq_free[zslot] = [("V", vM), ("S", sE)]
                if zslot == 1:
                    kpp_free = [("V", vM), ("S", sE)]
                gE = S.add("G", lambda g, hp=hp, zslot=zslot: g.tensor_scalar(
                    QPB[:, hp * 2:(hp + 1) * 2, :].rearrange("p h m -> p (h m)"),
                    KPF[:, zslot, :], INV16, FM_EPS16, ALU.mult, ALU.add),
                    waits=[("S", sE), qpb_free], inc="G")
                kpf_free[zslot] = ("G", gE)
                qpb_ready.append(("G", gE))
                drain(2)

            qfm_ready = []
            for rnd in range(4):
                tQt = None
                for k in range(8):
                    idx = rnd * 8 + k
                    h, mc = idx // 2, idx % 2
                    tQt = S.add("T", lambda t, h=h, mc=mc, k=k: t.transpose(
                        TRB[:, k * 128:(k + 1) * 128],
                        QPB[:, h, mc * 128:(mc + 1) * 128], IDB[:, :]),
                        waits=[qpb_ready[h // 2], trb_free, qfm_free] if k == 0
                              else [qpb_ready[h // 2]],
                        inc="T" if k == 7 else None)
                vQf = S.add("V", lambda v, rnd=rnd: v.tensor_copy(
                    QFM[:, 32 + rnd * 8:32 + (rnd + 1) * 8, :],
                    TRB[:, :].rearrange("p (a n) -> p a n", a=8)),
                    waits=[("T", tQt)], inc="V")
                trb_free = ("V", vQf)
                qfm_ready.append(("V", vQf))
                drain(2)
            qpb_free = ("T", S.cnt["T"])

            otm_ready = []
            for rnd in range(4):
                nslot = rnd % 2
                nbank = VPP if nslot == 0 else KPP
                nview = nbank[:, 0:260].rearrange("p (h m) -> p h m", h=4)
                extra = kpp_free if nslot == 1 else []
                tN = None
                for k in range(4):
                    h = rnd * 4 + k
                    for mc in range(2):
                        tN = S.add("T", lambda t, h=h, mc=mc, k=k, nview=nview: t.matmul(
                            nview[:, k, :], QFM[:, 32 + h * 2 + mc, :],
                            KVB[:, mc, h, 0:65],
                            start=(k == 0 and mc == 0), stop=(k == 3 and mc == 1)),
                            waits=([qfm_ready[rnd], kvb_ready, otm_free] +
                                   num_free[nslot] + extra)
                                  if (k == 0 and mc == 0) else [],
                            inc="T" if (k == 3 and mc == 1) else None)
                vD = S.add("V", lambda v, rnd=rnd, nview=nview: v.tensor_scalar(
                    DEN[:, rnd * 4:(rnd + 1) * 4], nview[:, :, 64:65],
                    DEN_EPS, None, ALU.add),
                    waits=[("T", tN)], inc="V")
                S.add("V", lambda v, rnd=rnd: v.reciprocal(
                    RDEN[:, rnd * 4:(rnd + 1) * 4], DEN[:, rnd * 4:(rnd + 1) * 4]),
                    waits=rt("V"), inc="V")
                vO = None
                for k in range(4):
                    h = rnd * 4 + k
                    vO = S.add("V", lambda v, h=h, k=k, nview=nview: v.tensor_scalar(
                        OTMB[:, h, :], nview[:, k, 0:64], RDEN[:, h:h + 1],
                        None, ALU.mult),
                        waits=rt("V") if k == 0 else [], inc="V")
                num_free[nslot] = [("V", vO)]
                if nslot == 1:
                    kpp_free = [("V", vO)]
                otm_ready.append(("V", vO))
                drain(2)
            qfm_free = ("T", S.cnt["T"])

            tOt = None
            for ccg in range(8):
                tOt = S.add("T", lambda t, ccg=ccg: t.transpose(
                    TRB[:, ccg * 128:(ccg + 1) * 128],
                    OTMB[:, 2 * ccg:2 * ccg + 2, :].rearrange("p h e -> p (h e)"),
                    IDB[:, :]),
                    waits=(otm_ready + [trb_free, ofm_free]) if ccg == 0 else [],
                    inc="T" if ccg == 7 else None)
            vOf = S.add("V", lambda v: v.tensor_copy(
                OFMb[:, :, :], TRB[:, :].rearrange("p (a b) -> p a b", a=8)),
                waits=[("T", tOt)], inc="V")
            trb_free = ("V", vOf)
            otm_free = ("T", S.cnt["T"])

            ob = {0: ZKP, 1: VPP}
            tO = None
            for cc in range(8):
                for hh in range(2):
                    tO = S.add("T", lambda t, cc=cc, hh=hh, ob=ob: t.matmul(
                        ob[hh][:, :], OFMb[:, cc, :],
                        WB[:, cc, hh * 512:(hh + 1) * 512],
                        start=(cc == 0), stop=False),
                        waits=([("V", vOf), ("W", wBO)] + zq_free[0] + num_free[0])
                              if (cc == 0 and hh == 0) else [], inc=None)
            for hh in range(2):
                tO = S.add("T", lambda t, hh=hh, ob=ob: t.matmul(
                    ob[hh][:, :], ONE1[:, :],
                    BOr[:, hh * 512:(hh + 1) * 512], start=False, stop=True),
                    inc="T" if hh == 1 else None)
            ofm_free = ("T", S.cnt["T"])

            ln_bnstats(sub, ZKP[:, :], VPP[:, :], waits=[("T", tO)])
            ln_aggr(sub)
            vn = ln_combine(sub, 1)
            sY0 = S.add("S", lambda s, sub=sub: s.activation(
                YN[:, 0:512], ZKP[:, :], AF.Identity,
                bias=LNm[:, sub:sub + 1], scale=LSc[:, sub:sub + 1]),
                waits=[("V", vn), yn_free], inc="S")
            sY1 = S.add("S", lambda s, sub=sub: s.activation(
                YN[:, 512:1024], VPP[:, :], AF.Identity,
                bias=LNm[:, sub:sub + 1], scale=LSc[:, sub:sub + 1]), inc="S")
            zq_free[0] = [("S", sY1)]
            num_free[0] = [("S", sY1)]
            gX1 = S.add("G", lambda g, sub=sub, slot=slot, X1B=X1B: g.tensor_tensor(
                X1B[:, sub, :], XT[:, slot, :], YN[:, :], ALU.add),
                waits=[("S", sY1), ("X", xw), x1_free[bt % 2]], inc="G")
            yn_free = ("G", gX1)
            xt_free[slot] = [("G", gX1)]
            if sub == 3:
                qf_free = ("T", S.cnt["T"])

        def emit_C(bt):
            nonlocal sel_free, xnf2_free, trb_free, oh_free
            X1B = X1Bp[bt % 2]
            st = {}
            gX1_last = ("G", S.cnt["G"])
            for sub in range(4):
                ln_bnstats(sub, X1B[:, sub, 0:512], X1B[:, sub, 512:1024],
                           waits=[gX1_last] if sub == 0 else [])
            for sub in range(4):
                ln_aggr(sub)
            vn2 = ln_combine(0, 4)
            xn2_parts = []
            xnf2_parts = []
            for sub in range(4):
                slot = sub % 2
                sN2 = S.add("S", lambda s, sub=sub, X1B=X1B: s.activation(
                    SEL[:, sub, :], X1B[:, sub, :], AF.Identity,
                    bias=LNm[:, sub:sub + 1], scale=LSc[:, sub:sub + 1]),
                    waits=[("V", vn2), sel_free], inc="S")
                xn2_parts.append(("S", sN2))
                sC = S.add("S", lambda s, sub=sub, slot=slot, X1B=X1B: s.activation(
                    XNB[:, slot, :], X1B[:, sub, :], AF.Identity,
                    bias=LNm[:, sub:sub + 1], scale=LSc[:, sub:sub + 1]),
                    waits=[xnb_free[slot]], inc="S")
                tT = None
                for ccg in range(8):
                    tT = S.add("T", lambda t, slot=slot, ccg=ccg: t.transpose(
                        TRB[:, ccg * 128:(ccg + 1) * 128],
                        XNB[:, slot, ccg * 128:(ccg + 1) * 128], IDB[:, :]),
                        waits=[("S", sC), trb_free, xnf2_free] if ccg == 0 else [],
                        inc="T" if ccg == 7 else None)
                xnb_free[slot] = ("T", tT)
                vX2 = S.add("V", lambda v, sub=sub: v.tensor_copy(
                    XNF2[:, :, sub, :], TRB[:, :].rearrange("p (a b) -> p a b", a=8)),
                    waits=[("T", tT)], inc="V")
                trb_free = ("V", vX2)
                xnf2_parts.append(("V", vX2))
            st["xn2"] = xn2_parts
            st["xnf2"] = xnf2_parts

            gate_groups = []
            for sub in range(4):
                tGt = None
                for cc in range(8):
                    tGt = S.add("T", lambda t, sub=sub, cc=cc: t.matmul(
                        ZEPf[:, sub * 8:(sub + 1) * 8], XNF2[:, cc, sub, :], GT[:, cc, :],
                        start=(cc == 0), stop=(cc == 7)),
                        waits=([xnf2_parts[sub], ("V", vGT)] + zep_free[0] + zep_free[1])
                              if (sub == 0 and cc == 0) else
                              ([xnf2_parts[sub]] if cc == 0 else []),
                        inc="T" if cc == 7 else None)
                gate_groups.append(tGt)
            for sub in range(4):
                S.add("V", lambda v, sub=sub: v.max(
                    GMX[:, sub, :], ZEPf[:, sub * 8:(sub + 1) * 8]),
                    waits=[("T", gate_groups[sub])] + (oh_free if sub == 0 else []),
                    inc="V")
            vg = S.cnt["V"]
            for sub in range(4):
                S.add("V", lambda v, sub=sub: v.max_index(
                    GIX[:, sub, :], GMX[:, sub, :], ZEPf[:, sub * 8:(sub + 1) * 8]),
                    waits=rt("V"), inc="V")
            for sub in range(4):
                S.add("V", lambda v, sub=sub: v.tensor_copy(
                    IXF[:, sub:sub + 1], GIX[:, sub, 0:1]),
                    waits=rt("V"), inc="V")
            vg = S.cnt["V"]
            vOH = None
            for sub in range(4):
                vOH = S.add("V", lambda v, sub=sub: v.tensor_scalar(
                    OH[:, sub, :], IO8[:, :], IXF[:, sub:sub + 1], None, ALU.is_equal),
                    waits=rt("V") + ([("G", gIot)] if sub == 0 else []), inc="V")
            st["oh"] = ("V", vOH)
            zep_free[0] = [("V", vOH)]
            st["last_sel"] = {}
            stC[bt] = st

        stDw = {}

        def emit_D_w(bt, r):
            e, fh = r // 2, r % 2
            wslot = r % 2
            stDw[r] = S.add("Y", lambda y, e=e, fh=fh, wslot=wslot: y.dma_start(
                out=WET[:, wslot, :, :],
                in_=wet_d[e, :, fh * 512:(fh + 1) * 512].rearrange(
                    "(hc p) f -> p hc f", p=128)),
                waits=[wet_free[wslot], ("P", yWET_done)], inc="E")

        def emit_D_g(bt, r, sub):
            e, fh = r // 2, r % 2
            st = stC[bt]
            wslot = r % 2
            wE = stDw[r]
            if True:
                sidx = (fh * 4 + sub) % 4
                zt, zhalf = (ZEPf, sidx % 2) if sidx < 2 else (OPPf, sidx % 2)
                bankfree = zep_free[zhalf] if sidx < 2 else opp_free[zhalf]
                tZe = None
                for cc in range(8):
                    tZe = S.add("T", lambda t, cc=cc, sub=sub, wslot=wslot,
                                zt=zt, zhalf=zhalf: t.matmul(
                        zt[:, zhalf * 512:(zhalf + 1) * 512],
                        XNF2[:, cc, sub, :], WET[:, wslot, cc, :],
                        start=(cc == 0), stop=(cc == 7)),
                        waits=([("E", wE), st["xnf2"][sub], st["oh"]] + bankfree)
                              if cc == 0 else [], inc="T" if cc == 7 else None)
                selw = [("T", tZe)]
                if e > 0:
                    selw.append(st["last_sel"][(sub, fh)])
                else:
                    selw.append(st["xn2"][sub])
                src1 = SEL[:, sub, fh * 512:(fh + 1) * 512]
                dst = SEL[:, sub, fh * 512:(fh + 1) * 512]
                zsrc = zt[:, zhalf * 512:(zhalf + 1) * 512]
                ohap = OH[:, sub, e:e + 1]
                if sidx % 2 == 0:
                    sv = S.add("V", lambda v, dst=dst, zsrc=zsrc, ohap=ohap,
                               src1=src1: v.scalar_tensor_tensor(
                                   dst, zsrc, ohap, src1, ALU.mult, ALU.add),
                               waits=selw, inc="V")
                    tok = ("V", sv)
                    bank_tok = tok
                else:
                    pslot = sub % 2
                    s1 = S.add("S", lambda s, zsrc=zsrc, ohap=ohap, pslot=pslot:
                               s.activation(STMP[:, pslot, :], zsrc, AF.Identity,
                                            bias=0.0, scale=ohap),
                               waits=[("T", tZe), stmp_free[pslot]], inc="S")
                    g1 = S.add("G", lambda g, dst=dst, src1=src1, pslot=pslot:
                               g.tensor_tensor(dst, src1, STMP[:, pslot, :], ALU.add),
                               waits=[("S", s1)] + selw[1:], inc="G")
                    stmp_free[pslot] = ("G", g1)
                    tok = ("G", g1)
                    bank_tok = ("S", s1)
                st["last_sel"][(sub, fh)] = tok
                if sidx < 2:
                    zep_free[zhalf] = [bank_tok]
                else:
                    opp_free[zhalf] = [bank_tok]
            if sub == 3:
                wet_free[wslot] = ("T", S.cnt["T"])

        def emit_D_tail(bt):
            nonlocal sel_free, xnf2_free, yn2_free, oh_free
            X1B = X1Bp[bt % 2]
            st = stC[bt]
            last_sel = st["last_sel"]
            for sub in range(4):
                ln_bnstats(sub, SEL[:, sub, 0:512], SEL[:, sub, 512:1024],
                           waits=[last_sel[(sub, 0)], last_sel[(sub, 1)]])
            for sub in range(4):
                ln_aggr(sub)
            vn3 = ln_combine(0, 4)
            gOut = None
            for sub in range(4):
                sY2 = S.add("S", lambda s, sub=sub: s.activation(
                    YN2[:, :], SEL[:, sub, :], AF.Identity,
                    bias=LNm[:, sub:sub + 1], scale=LSc[:, sub:sub + 1]),
                    waits=[("V", vn3), yn2_free], inc="S")
                gOut = S.add("G", lambda g, sub=sub, X1B=X1B: g.tensor_tensor(
                    X1B[:, sub, :], X1B[:, sub, :], YN2[:, :], ALU.add),
                    waits=[("S", sY2)], inc="G")
                yn2_free = ("G", gOut)
            yO = S.add("Y", lambda y, bt=bt, X1B=X1B: y.dma_start(
                out=out_d[bt * 512:(bt + 1) * 512, :].rearrange("(s p) c -> p s c", p=128),
                in_=X1B[:, :, :]),
                waits=[("G", gOut)], inc="O")
            x1_free[bt % 2] = ("O", yO)
            sel_free = ("O", yO)
            xnf2_free = ("T", S.cnt["T"])
            oh_free = list(last_sel.values())

        yn2_free = ("G", 0)
        pending = []

        def drain(k):
            for _ in range(min(k, len(pending))):
                kind, b, r, s = pending.pop(0)
                if kind == "w":
                    emit_D_w(b, r)
                else:
                    emit_D_g(b, r, s)

        def drain_all():
            drain(len(pending))

        for bt in range(NB2):
            emit_A(bt)
            if bt > 0:
                for r in range(16):
                    pending.append(("w", bt - 1, r, 0))
                    for s in range(4):
                        pending.append(("g", bt - 1, r, s))
            for sub in range(4):
                emit_B_sub(bt, sub, drain)
            drain_all()
            if bt > 0:
                emit_D_tail(bt - 1)
            emit_C(bt)
        for r in range(16):
            emit_D_w(NB2 - 1, r)
            for s in range(4):
                emit_D_g(NB2 - 1, r, s)
        emit_D_tail(NB2 - 1)

        if dbg:
            allw = [("T", S.cnt["T"]), ("V", S.cnt["V"]), ("S", S.cnt["S"]),
                    ("G", S.cnt["G"])]
            dbg_list = [
                ("d_STs", STs, F32), ("d_STn", STn, F32),
                ("d_KVB", KVB, BF16), ("d_QF", QF, BF16),
                ("d_KPB", KPB, BF16), ("d_OTMB", OTMB, BF16),
                ("d_OH", OH, F32),
                ("d_SEL", KVS, F32), ("d_DEN", DEN, F32),
                ("d_XNF", XNF, BF16), ("d_X1B", X1Ba, F32),
                ("d_PD", PD, BF16), ("d_GT", GT, BF16),
                ("d_VE", VE, BF16), ("d_KF", KF, BF16),
                ("d_BQc", BQc, F32), ("d_BKc", BKc, F32),
                ("d_RDEN", RDEN, F32), ("d_MV", MV, F32),
                ("d_LSc", LSc, F32), ("d_QFb", QF, BF16),
            ]
            for nm, buf, dt in dbg_list:
                shp = [128] + [int(s) for s in buf.shape[1:]]
                dd = nc.dram_tensor(nm, shp, dt, kind="ExternalOutput").ap()
                S.add("Y", lambda y, dd=dd, buf=buf: y.dma_start(out=dd[:], in_=buf[:]),
                      waits=allw, inc="O")
        S.add("Y", lambda y: y.nop(), waits=[("O", S.cnt["O"])])

        with nc.Block() as block:
            @block.tensor
            def _(t):
                S.emit(t, "T", sems)

            @block.vector
            def _(v):
                S.emit(v, "V", sems)

            @block.scalar
            def _(s):
                S.emit(s, "S", sems)

            @block.gpsimd
            def _(g):
                S.emit(g, "G", sems)

            @block.sync
            def _(y):
                S.emit(y, "Y", sems)

    return nc


def kernel(L0_attn_x, L0_attn_w_qkv, L0_attn_b_qkv, L0_attn_w_out, L0_attn_b_out,
           L0_attn_proj_matrix, L0_ffn_gate_w, L0_ffn_expert_w):
    x = np.ascontiguousarray(np.asarray(L0_attn_x, dtype=np.float32))
    wqkv = np.ascontiguousarray(np.asarray(L0_attn_w_qkv, dtype=np.float32))
    bqkv = np.ascontiguousarray(np.asarray(L0_attn_b_qkv, dtype=np.float32))
    wout = np.ascontiguousarray(np.asarray(L0_attn_w_out, dtype=np.float32))
    bout = np.ascontiguousarray(np.asarray(L0_attn_b_out, dtype=np.float32))
    proj = np.ascontiguousarray(np.asarray(L0_attn_proj_matrix, dtype=np.float32))
    gate = np.ascontiguousarray(np.asarray(L0_ffn_gate_w, dtype=np.float32))
    expw = np.ascontiguousarray(np.asarray(L0_ffn_expert_w, dtype=np.float32))

    nc = build_program()
    in_maps = []
    for c in range(8):
        b, hf = c // 2, c % 2
        own = x[b, hf * 4096:(hf + 1) * 4096]
        other = x[b, (1 - hf) * 4096:(2 - hf) * 4096]
        xp = np.ascontiguousarray(np.concatenate([own, other], axis=0))
        in_maps.append({
            "x_seq": xp, "w_qkv": wqkv, "b_qkv": bqkv, "w_out": wout,
            "b_out": bout, "proj": proj, "gate_w": gate, "expert_w": expw,
        })
    res = run_bass_kernel_spmd(nc, in_maps, core_ids=list(range(8)))
    out = np.empty((4, 8192, C), dtype=np.float32)
    for c in range(8):
        b, hf = c // 2, c % 2
        out[b, hf * 4096:(hf + 1) * 4096] = res.results[c]["out"]
    return out
